# version 06
# baseline (speedup 1.0000x reference)
import sys
import math
import numpy as np

try:
    import concourse.bass as bass
except ImportError:
    sys.path.insert(0, "/opt/trn_rl_repo")
    import concourse.bass as bass
import concourse.mybir as mybir
from concourse.bass_utils import run_bass_kernel_spmd
from contextlib import ExitStack

F32 = mybir.dt.float32
BF16 = mybir.dt.bfloat16
AF = mybir.ActivationFunctionType
ALU = mybir.AluOpType
AX = mybir.AxisListType

H = 1024
D = 64
NE = 32
IEXP = 512
TOPK = 8
NGRP = 4
RSCAL = 2.5
NHC = H // 128
EXP_SHIFT = -16.0


def _alibi_slopes(num_heads):
    closest_pow2 = 2 ** int(math.floor(math.log2(num_heads)))
    base = 2.0 ** (-(2.0 ** (-(math.log2(closest_pow2) - 3))))
    powers = np.arange(1, closest_pow2 + 1, dtype=np.float32)
    slopes = base ** powers
    if closest_pow2 != num_heads:
        extra_base = 2.0 ** (-(2.0 ** (-(math.log2(2 * closest_pow2) - 3))))
        extra_powers = np.arange(1, 2 * (num_heads - closest_pow2) + 1, 2, dtype=np.float32)
        slopes = np.concatenate([slopes, extra_base ** extra_powers])
    return slopes[:num_heads].astype(np.float32)

SLOPES = _alibi_slopes(32)

ENGINES = ("sync", "vector", "scalar", "tensor", "gpsimd")
NONE = None


class Sem:
    def __init__(self, es, nc, name):
        self.h = es.enter_context(nc.semaphore(name))
        self.n = 0
        self.name = name


class Plan:
    def __init__(self):
        self.ops = []

    def add(self, eng, fn, waits=(), inc=None, inck=1):
        ws = [w for w in waits if w is not None and w[1] > 0]
        incs = []
        tok = None
        if inc is not None:
            inc.n += inck
            incs.append((inc, inck))
            tok = (inc, inc.n)
        self.ops.append((eng, fn, ws, incs))
        return tok

    def emit(self, blk):
        def mk(name):
            def f(eng):
                waited = {}
                for (e, fn, ws, incs) in self.ops:
                    if e != name:
                        continue
                    for (s, v) in ws:
                        if waited.get(s.name, 0) >= v:
                            continue
                        eng.wait_ge(s.h, v)
                        waited[s.name] = v
                    last = fn(eng)
                    for (s, k) in incs:
                        last.then_inc(s.h, k)
            return f
        for name in ENGINES:
            getattr(blk, name)(mk(name))


def build_program(S=2048, n_exp=34, n_pack=16, exp_ids=None):
    NTC = S // 128
    NQB = S // 512
    NTBP = S // 512
    nc = bass.Bass("TRN2", target_bir_lowering=False)
    es = ExitStack()

    x_d = nc.dram_tensor("x", [S, H], F32, kind="ExternalInput").ap()
    wqkv_d = nc.dram_tensor("wqkv", [H, 4096], F32, kind="ExternalInput").ap()
    wout_d = nc.dram_tensor("wout", [2048, H], F32, kind="ExternalInput").ap()
    gwT_d = nc.dram_tensor("gwT", [H, 32], F32, kind="ExternalInput").ap()
    gb_d = nc.dram_tensor("gb", [1, 32], F32, kind="ExternalInput").ap()
    eg_d = nc.dram_tensor("eg", [34, H, IEXP], F32, kind="ExternalInput").ap()
    eu_d = nc.dram_tensor("eu", [34, H, IEXP], F32, kind="ExternalInput").ap()
    ed_d = nc.dram_tensor("ed", [34, IEXP, H], F32, kind="ExternalInput").ap()
    out_d = nc.dram_tensor("out", [S, H], F32, kind="ExternalOutput").ap()

    xnh = es.enter_context(nc.sbuf_tensor("xnh", [128, NHC * S], BF16))
    xnl = es.enter_context(nc.sbuf_tensor("xnl", [128, NHC * S], BF16))
    oa = es.enter_context(nc.sbuf_tensor("oa", [128, NTC * H], F32))
    wde = es.enter_context(nc.sbuf_tensor("wde", [128, NTC * 32], F32))
    onesP = es.enter_context(nc.sbuf_tensor("onesP", [128, 128], F32))
    stg2 = [es.enter_context(nc.sbuf_tensor("stg2%d" % i, [128, 512], F32)) for i in range(2)]
    bexp = es.enter_context(nc.sbuf_tensor("bexp", [128, 1], F32))

    psA = [es.enter_context(nc.psum_tensor("psA%d" % i, [128, 512], F32)) for i in range(4)]
    psB = [es.enter_context(nc.psum_tensor("psB%d" % i, [128, 1024], F32)) for i in range(2)]

    s_x = [Sem(es, nc, "s_x0"), Sem(es, nc, "s_x1")]
    s_gw = Sem(es, nc, "s_gw")
    s_vg = Sem(es, nc, "s_vg")
    s_wq = Sem(es, nc, "s_wq")
    s_wo = [Sem(es, nc, "s_wo0"), Sem(es, nc, "s_wo1")]
    s_wg = Sem(es, nc, "s_wg")
    s_wu = Sem(es, nc, "s_wu")
    s_wd = Sem(es, nc, "s_wd")
    s_gh = Sem(es, nc, "s_gh")
    s_qh = Sem(es, nc, "s_qh")
    s_vh = Sem(es, nc, "s_vh")
    s_uh = Sem(es, nc, "s_uh")
    s_dh = Sem(es, nc, "s_dh")
    s_o = [Sem(es, nc, "s_o0"), Sem(es, nc, "s_o1")]
    s_st2 = [Sem(es, nc, "s_st20"), Sem(es, nc, "s_st21")]
    ve = Sem(es, nc, "ve")
    sc = Sem(es, nc, "sc")
    te = Sem(es, nc, "te")
    gp = Sem(es, nc, "gp")

    p = Plan()

    def mm3(t, out_ap, wh, wl, xh, xl, first, last):
        t.matmul(out_ap, wh, xh, start=first, stop=False)
        t.matmul(out_ap, wh, xl, start=False, stop=False)
        return t.matmul(out_ap, wl, xh, start=False, stop=last)


    t_bexp = p.add("vector", lambda v: v.memset(bexp[:], EXP_SHIFT), inc=ve)
    t_oaz = p.add("vector", lambda v: v.memset(oa[:], 0.0), inc=ve)
    t_ones = p.add("vector", lambda v: v.memset(onesP[:], 1.0), inc=ve)

    es_a = ExitStack()
    xt = [es_a.enter_context(nc.sbuf_tensor("xt%d" % i, [128, H], F32)) for i in range(2)]
    lt = es_a.enter_context(nc.sbuf_tensor("lt", [128, H], F32))
    sqt = es_a.enter_context(nc.sbuf_tensor("sqt", [128, H], F32))
    lns = es_a.enter_context(nc.sbuf_tensor("lns", [128, 8], F32))
    xnt = [es_a.enter_context(nc.sbuf_tensor("xnt%d" % i, [128, H], F32)) for i in range(2)]
    itmp = es_a.enter_context(nc.sbuf_tensor("itmp", [128, 128], F32))
    ident = es_a.enter_context(nc.sbuf_tensor("ident", [128, 128], F32))

    t_ii = p.add("gpsimd", lambda g: g.iota(itmp[:], pattern=[[1, 128]], base=127,
                                            channel_multiplier=-1,
                                            allow_small_or_imprecise_dtypes=True), inc=gp)
    t_id = p.add("vector", lambda v: v.tensor_scalar(ident[:], itmp[:], 127.0, None, ALU.is_equal),
                 waits=[t_ii], inc=ve)

    gwTs = es_a.enter_context(nc.sbuf_tensor("gwTs", [128, NHC * 32], F32))
    gwh = es_a.enter_context(nc.sbuf_tensor("gwh", [128, NHC * 32], BF16))
    gwl = es_a.enter_context(nc.sbuf_tensor("gwl", [128, NHC * 32], BF16))
    gbr = es_a.enter_context(nc.sbuf_tensor("gbr", [1, 32], F32))
    gbb = es_a.enter_context(nc.sbuf_tensor("gbb", [128, 32], F32))
    gsc = es_a.enter_context(nc.sbuf_tensor("gsc", [128, 256], F32))
    sco = es_a.enter_context(nc.sbuf_tensor("sco", [128, NTC * 32], F32))

    t_w1 = p.add("sync", lambda sy: sy.dma_start(
        out=gwTs.ap().rearrange("p (c e) -> p c e", c=NHC),
        in_=gwT_d.rearrange("(c p) e -> p c e", p=128)), inc=s_gw, inck=16)
    t_w2 = p.add("sync", lambda sy: sy.dma_start(out=gbr[:], in_=gb_d), inc=s_gw, inck=16)
    t_w2 = (s_gw, s_gw.n)

    t_gsh = p.add("vector", lambda v: v.tensor_copy(gwh[:], gwTs[:]), waits=[t_w2], inc=ve)
    t_gsl = p.add("vector", lambda v: v.scalar_tensor_tensor(gwl[:], gwTs[:], 1.0, gwh[:],
                                                             ALU.mult, ALU.subtract),
                  waits=[t_gsh], inc=ve)
    t_bb = p.add("tensor", lambda t: t.matmul(psA[3][:, 0:32], onesP[0:1, 0:128], gbr[:], start=True, stop=True),
                 waits=[t_w2, t_ones], inc=te)
    t_gbb = p.add("vector", lambda v: v.tensor_copy(gbb[:], psA[3][:, 0:32]), waits=[t_bb], inc=ve)
    psA3_free0 = t_bb_evac_token = t_gbb


    sig_done = [NONE] * NTC
    psA_gfree = [NONE, NONE]

    def emit_gate(tc, xn_tok):
        def mmg(t, tc=tc):
            ins = None
            for c in range(NHC):
                ins = mm3(t, psA[tc % 2][:, 0:32],
                          xnh[:, c * S + tc * 128:c * S + tc * 128 + 128],
                          xnl[:, c * S + tc * 128:c * S + tc * 128 + 128],
                          gwh[:, c * 32:(c + 1) * 32], gwl[:, c * 32:(c + 1) * 32],
                          c == 0, c == NHC - 1)
            return ins
        t_g = p.add("tensor", mmg, waits=[xn_tok, t_gsl, psA_gfree[tc % 2]], inc=te)
        c_sig = p.add("scalar", lambda s, tc=tc: s.activation(
            sco[:, tc * 32:(tc + 1) * 32], psA[tc % 2][:, 0:32], AF.Sigmoid),
            waits=[t_g], inc=sc)
        sig_done[tc] = c_sig
        psA_gfree[tc % 2] = c_sig

        scox = sco[:, tc * 32:(tc + 1) * 32]
        sct = gsc[:, 0:32]
        gs4 = gsc[:, 32:36]
        msk = gsc[:, 36:68]
        msc = gsc[:, 68:100]
        sel = gsc[:, 100:132]
        rmk = gsc[:, 132:164]
        m1 = gsc[:, 164:165]
        t8 = gsc[:, 165:173]
        m2 = gsc[:, 173:174]
        gm = gsc[:, 174:175]
        mk1 = gsc[:, 175:179]
        gq2 = gsc[:, 179:183]
        wr = gsc[:, 183:215]
        dsm = gsc[:, 215:216]
        prev = [NONE]

        def ch(fn, extra=()):
            prev[0] = p.add("vector", fn, waits=[prev[0]] + list(extra), inc=ve)
            return prev[0]

        ch(lambda v, scox=scox, sct=sct: v.tensor_tensor(sct, scox, gbb[:], ALU.add), extra=[c_sig])
        for g in range(NGRP):
            sl_ = sct[:, 8 * g:8 * g + 8]
            ch(lambda v, sl_=sl_, m1=m1: v.tensor_reduce(m1, sl_, AX.X, ALU.max))
            ch(lambda v, sl_=sl_, t8=t8, m1=m1: v.tensor_scalar(t8, sl_, m1, None, ALU.is_equal))
            ch(lambda v, sl_=sl_, t8=t8: v.scalar_tensor_tensor(t8, t8, -1e30, sl_, ALU.mult, ALU.add))
            ch(lambda v, t8=t8, m2=m2: v.tensor_reduce(m2, t8, AX.X, ALU.max))
            ch(lambda v, g=g, gs4=gs4, m1=m1, m2=m2: v.tensor_tensor(gs4[:, g:g + 1], m1, m2, ALU.add))
        ch(lambda v, gm=gm, gs4=gs4: v.tensor_reduce(gm, gs4, AX.X, ALU.max))
        ch(lambda v, mk1=mk1, gs4=gs4, gm=gm: v.tensor_scalar(mk1, gs4, gm, None, ALU.is_equal))
        ch(lambda v, gq2=gq2, mk1=mk1, gs4=gs4: v.scalar_tensor_tensor(gq2, mk1, -1e30, gs4, ALU.mult, ALU.add))
        ch(lambda v, gm=gm, gq2=gq2: v.tensor_reduce(gm, gq2, AX.X, ALU.max))
        ch(lambda v, gq2=gq2, gm=gm: v.tensor_scalar(gq2, gq2, gm, None, ALU.is_equal))
        ch(lambda v, mk1=mk1, gq2=gq2: v.tensor_tensor(mk1, mk1, gq2, ALU.add))
        ch(lambda v, msk=msk, mk1=mk1: v.tensor_copy(msk.rearrange("p (g o) -> p g o", o=8),
                                                     mk1.rearrange("p (g o) -> p g o", o=1).broadcast_to([128, 4, 8])))
        ch(lambda v, msc=msc, sct=sct, msk=msk: v.tensor_mul(msc, sct, msk))
        ch(lambda v, sel=sel: v.memset(sel, 0.0))
        for r in range(TOPK):
            ch(lambda v, m1=m1, msc=msc: v.tensor_reduce(m1, msc, AX.X, ALU.max))
            ch(lambda v, rmk=rmk, msc=msc, m1=m1: v.tensor_scalar(rmk, msc, m1, None, ALU.is_equal))
            ch(lambda v, sel=sel, rmk=rmk: v.tensor_tensor(sel, sel, rmk, ALU.add))
            ch(lambda v, msc=msc, rmk=rmk: v.scalar_tensor_tensor(msc, rmk, -1e30, msc, ALU.mult, ALU.add))
        ch(lambda v, wr=wr, scox=scox, sel=sel: v.tensor_mul(wr, scox, sel))
        ch(lambda v, dsm=dsm, wr=wr: v.tensor_reduce(dsm, wr, AX.X, ALU.add))
        ch(lambda v, dsm=dsm: v.tensor_scalar(dsm, dsm, 1.0, 1e-20, ALU.mult, ALU.add))
        ch(lambda v, m1=m1, dsm=dsm: v.reciprocal(m1, dsm))
        ch(lambda v, m1=m1: v.tensor_scalar(m1, m1, RSCAL, None, ALU.mult))
        ch(lambda v, tc=tc, wr=wr, m1=m1: v.tensor_scalar(wde[:, tc * 32:(tc + 1) * 32], wr, m1, None, ALU.mult))

    x_free = [NONE, NONE]
    xnt_free = [NONE, NONE]
    for tc in range(NTC):
        b = tc % 2
        t_x = p.add("sync", lambda sy, tc=tc, b=b: sy.dma_start(
            out=xt[b][:], in_=x_d[tc * 128:(tc + 1) * 128, :]),
            waits=[x_free[b]], inc=s_x[b], inck=16)

        prev = [NONE]

        def ch(fn, extra=(), eng="vector", sem=ve):
            prev[0] = p.add(eng, fn, waits=[prev[0]] + list(extra), inc=sem)
            return prev[0]

        ch(lambda v, b=b: v.tensor_reduce(lns[:, 0:1], xt[b][:], AX.X, ALU.add), extra=[t_x])
        ch(lambda v: v.tensor_scalar(lns[:, 1:2], lns[:, 0:1], 1.0 / H, None, ALU.mult))
        x_free[b] = ch(lambda v, b=b: v.tensor_scalar(lt[:], xt[b][:], lns[:, 1:2], None, ALU.subtract))
        ch(lambda v: v.tensor_mul(sqt[:], lt[:], lt[:]))
        ch(lambda v: v.tensor_reduce(lns[:, 2:3], sqt[:], AX.X, ALU.add))
        t6 = ch(lambda v: v.tensor_scalar(lns[:, 3:4], lns[:, 2:3], 1.0 / H, 1e-5, ALU.mult, ALU.add))
        t7 = p.add("scalar", lambda s: s.activation(lns[:, 4:5], lns[:, 3:4], AF.Sqrt),
                   waits=[t6], inc=sc)
        ch(lambda v: v.reciprocal(lns[:, 5:6], lns[:, 4:5]), extra=[t7])
        ch(lambda v: v.tensor_mul(lns[:, 6:7], lns[:, 3:4], lns[:, 5:6]))
        ch(lambda v: v.tensor_tensor(lns[:, 7:8], lns[:, 4:5], lns[:, 6:7], ALU.add))
        ch(lambda v: v.tensor_scalar(lns[:, 6:7], lns[:, 7:8], 0.5, None, ALU.mult))
        ch(lambda v: v.reciprocal(lns[:, 7:8], lns[:, 6:7]))
        t12 = ch(lambda v, b=b: v.tensor_scalar(xnt[b][:], lt[:], lns[:, 7:8], None, ALU.mult),
                 extra=[xnt_free[b]])
        for c2 in range(NHC // 2):
            def tp(t, b=b, c2=c2):
                t.transpose(psA[3][:, 0:128], xnt[b][:, (2 * c2) * 128:(2 * c2) * 128 + 128], ident[:])
                return t.transpose(psA[3][:, 128:256], xnt[b][:, (2 * c2 + 1) * 128:(2 * c2 + 1) * 128 + 128], ident[:])
            t_tp = p.add("tensor", tp, waits=[t12, psA3_free0, t_id], inc=te)
            xnt_free[b] = t_tp

            def evh(v, tc=tc, c2=c2):
                for k2 in range(2):
                    dst = (2 * c2 + k2) * S + tc * 128
                    ins = v.tensor_copy(xnh[:, dst:dst + 128], psA[3][:, k2 * 128:k2 * 128 + 128])
                return ins
            t_evh = p.add("vector", evh, waits=[t_tp], inc=ve)

            def evl(v, tc=tc, c2=c2):
                for k2 in range(2):
                    dst = (2 * c2 + k2) * S + tc * 128
                    ins = v.scalar_tensor_tensor(xnl[:, dst:dst + 128], psA[3][:, k2 * 128:k2 * 128 + 128],
                                                 1.0, xnh[:, dst:dst + 128], ALU.mult, ALU.subtract)
                return ins
            psA3_free0 = p.add("vector", evl, waits=[t_evh], inc=ve)
        emit_gate(tc, psA3_free0)
    xn_ready = psA3_free0
    wde_ready = (ve, ve.n)
    psA3_free = psA3_free0



    es_a.close()

    def barrier():
        return [(ve, ve.n), (sc, sc.n), (te, te.n), (gp, gp.n)]

    barA = barrier()

    es_b = ExitStack()
    QeAh = es_b.enter_context(nc.sbuf_tensor("QeAh", [66, S], BF16))
    QeAl = es_b.enter_context(nc.sbuf_tensor("QeAl", [66, S], BF16))
    QeBh = es_b.enter_context(nc.sbuf_tensor("QeBh", [66, S], BF16))
    QeBl = es_b.enter_context(nc.sbuf_tensor("QeBl", [66, S], BF16))
    Keh = es_b.enter_context(nc.sbuf_tensor("Keh", [66, S], BF16))
    Kel = es_b.enter_context(nc.sbuf_tensor("Kel", [66, S], BF16))
    dcol = es_b.enter_context(nc.sbuf_tensor("dcol", [128, 19], F32))
    btA = es_b.enter_context(nc.sbuf_tensor("btA", [128, 19], F32))
    btB = es_b.enter_context(nc.sbuf_tensor("btB", [128, 19], F32))
    aftmp = es_b.enter_context(nc.sbuf_tensor("aftmp", [64, 512], F32))
    qpat = es_b.enter_context(nc.sbuf_tensor("qpat", [128, 512], F32))
    trim = es_b.enter_context(nc.sbuf_tensor("trim", [128, 128], F32))
    itm2 = es_b.enter_context(nc.sbuf_tensor("itm2", [128, 128], F32))
    qsv = [es_b.enter_context(nc.sbuf_tensor("qsv%d" % i, [128, 1], F32)) for i in range(2)]
    Vp = es_b.enter_context(nc.sbuf_tensor("Vp", [128, NTC * 130], F32))
    vgh = es_b.enter_context(nc.sbuf_tensor("vgh", [128, NHC * 128], BF16))
    vgl = es_b.enter_context(nc.sbuf_tensor("vgl", [128, NHC * 128], BF16))
    wqh = es_b.enter_context(nc.sbuf_tensor("wqh", [128, 8 * 192], BF16))
    wql = es_b.enter_context(nc.sbuf_tensor("wql", [128, 8 * 192], BF16))
    PT = [es_b.enter_context(nc.sbuf_tensor("PT%d" % i, [128, 512], F32)) for i in range(2)]
    afh2 = [[es_b.enter_context(nc.sbuf_tensor("afh%d_%d" % (i, j), [64, 512], BF16))
             for i in range(2)] for j in range(2)]
    afl2 = [[es_b.enter_context(nc.sbuf_tensor("afl%d_%d" % (i, j), [64, 512], BF16))
             for i in range(2)] for j in range(2)]
    lb2 = es_b.enter_context(nc.sbuf_tensor("lb2", [128, 512], F32))
    woAh = es_b.enter_context(nc.sbuf_tensor("woAh", [64, H], BF16))
    woAl = es_b.enter_context(nc.sbuf_tensor("woAl", [64, H], BF16))
    woBh = es_b.enter_context(nc.sbuf_tensor("woBh", [64, H], BF16))
    woBl = es_b.enter_context(nc.sbuf_tensor("woBl", [64, H], BF16))

    t_i2 = p.add("gpsimd", lambda g: g.iota(itm2[:], pattern=[[1, 128]], base=127,
                                            channel_multiplier=-1,
                                            allow_small_or_imprecise_dtypes=True),
                 waits=barA, inc=gp)
    t_msk = p.add("vector", lambda v: v.tensor_scalar(trim[:], itm2[:], 127.0, -1e30,
                                                      ALU.is_lt, ALU.mult),
                  waits=[t_i2], inc=ve)

    t_km = p.add("vector", lambda v: (v.memset(Keh[64:66, :], 1.0),
                                      v.memset(Kel[64:66, :], 0.0),
                                      v.memset(qpat[64:66, :], 1.0))[-1], waits=barA, inc=ve)
    t_ki = p.add("gpsimd", lambda g: (
        g.iota(Keh[64:65, :], pattern=[[0, S // 128], [1, 128]], base=0,
               channel_multiplier=0, allow_small_or_imprecise_dtypes=True),
        g.iota(qpat[64:66, 0:512], pattern=[[1, 512]], base=0,
               channel_multiplier=0, allow_small_or_imprecise_dtypes=True),
        g.iota(dcol[:], pattern=[[1, 19]], base=0, channel_multiplier=0,
               allow_small_or_imprecise_dtypes=True))[-1],
                 waits=[t_km], inc=gp)
    t_kmi = p.add("vector", lambda v: v.memset(qpat[64:65, 0:512], 1.0), waits=[t_ki], inc=ve)
    qpat_done = NONE
    t_vones = p.add("vector", lambda v: v.memset(
        Vp.ap().rearrange("p (t h o) -> p t h o", t=NTC, h=2)[:, :, :, 64:65], 1.0), inc=ve)

    stg_free = [NONE, NONE]
    vp_ready = [NONE] * NTC
    attn_rows_free = NONE
    afq_free = [[NONE, NONE], [NONE, NONE]]
    afq_ready_q = [NONE, NONE]
    PT_free = [NONE, NONE]
    psA01_free = [NONE, NONE]
    psA2_free = NONE
    pv_psA2_wait = [NONE]
    psB_free = [t_oaz, t_oaz]
    oa_last = [t_oaz] * NTC
    vgw_free = NONE
    vgs_free = NONE
    wq_free = NONE
    wqs_free = NONE
    wo_free = NONE
    wo_ready = NONE
    slope_rdy = [NONE, NONE]
    bt_ready = [NONE, NONE]
    bt_free = [NONE, NONE]
    lrow_free = NONE
    op_pend = [None]
    op_last = [NONE]
    aftmp_free = NONE

    for P in range(n_pack):
        pg4 = P // 4
        if P % 2 == 0:
            pg2 = P // 2
            vsrc = wqkv_d[:, (pg2 // 2) * 1024 + (pg2 % 2) * 128:(pg2 // 2) * 1024 + (pg2 % 2) * 128 + 128]
            t_vh = p.add("gpsimd", lambda gg, vsrc=vsrc: gg.dma_start(
                out=vgh.ap().rearrange("p (c k) -> p c k", c=NHC),
                in_=vsrc.rearrange("(c p) k -> p c k", p=128)),
                waits=[vgs_free] + barA, inc=s_vh, inck=16)
            t_vsl = NONE
            for vq in range(2):
                sb = vq % 2
                t_vs = p.add("sync", lambda sy, vsrc=vsrc, vq=vq, sb=sb: sy.dma_start(
                    out=stg2[sb].ap().rearrange("p (c k) -> p c k", c=4),
                    in_=vsrc.rearrange("(c p) k -> p c k", p=128)[:, vq * 4:(vq + 1) * 4, :]),
                    waits=[stg_free[sb]] + barA, inc=s_st2[sb], inck=16)
                t_vsl = p.add("gpsimd", lambda gg, vq=vq, sb=sb: gg.tensor_sub(
                    vgl[:, vq * 512:(vq + 1) * 512], stg2[sb][:],
                    vgh[:, vq * 512:(vq + 1) * 512]),
                    waits=[t_vs, t_vh, vgs_free, t_vsl], inc=gp)
                stg_free[sb] = t_vsl
            vgw_free = t_vsl
            last_tv = NONE
            for tc in range(NTC):
                def mmv(t, tc=tc):
                    ins = None
                    for c in range(NHC):
                        ins = mm3(t, psA[2][:, 0:128],
                                  xnh[:, c * S + tc * 128:c * S + tc * 128 + 128],
                                  xnl[:, c * S + tc * 128:c * S + tc * 128 + 128],
                                  vgh[:, c * 128:(c + 1) * 128], vgl[:, c * 128:(c + 1) * 128],
                                  c == 0, c == NHC - 1)
                    return ins
                t_v = p.add("tensor", mmv, waits=[xn_ready, vgw_free, pv_psA2_wait[0]], inc=te)
                last_tv = t_v

                def evv(v, tc=tc):
                    dst = Vp[:, tc * 130:tc * 130 + 130].rearrange("p (h o) -> p h o", h=2)[:, :, 0:64]
                    src = psA[2][:, 0:128].rearrange("p (h o) -> p h o", h=2)
                    return v.tensor_copy(dst, src)
                psA2_free = p.add("vector", evv, waits=[t_v, t_vones], inc=ve)
                pv_psA2_wait[0] = psA2_free
                vp_ready[tc] = psA2_free
            vgs_free = last_tv

        qsrc = wqkv_d[:, pg4 * 1024 + 256 + (P % 4) * 192:pg4 * 1024 + 256 + (P % 4) * 192 + 192]
        t_qh = p.add("gpsimd", lambda gg, qsrc=qsrc: gg.dma_start(
            out=wqh.ap().rearrange("p (c k) -> p c k", c=NHC),
            in_=qsrc.rearrange("(c p) k -> p c k", p=128)),
            waits=[wqs_free] + barA, inc=s_qh, inck=16)
        wq_lo = NONE
        for qh in range(3):
            sb = qh % 2
            t_qs = p.add("sync", lambda sy, qsrc=qsrc, qh=qh, sb=sb: sy.dma_start(
                out=stg2[sb].ap().rearrange("p (c k) -> p c k", c=8),
                in_=qsrc.rearrange("(c p) (q k) -> p c q k", p=128, q=3)[:, :, qh, :]),
                waits=[stg_free[sb]] + barA, inc=s_st2[sb], inck=16)
            wq_lo = p.add("gpsimd", lambda gg, qh=qh, sb=sb: gg.tensor_sub(
                wql.ap().rearrange("p (c k) -> p c k", c=NHC)
                .rearrange("p c (q k) -> p c q k", q=3)[:, :, qh, :],
                stg2[sb][:].rearrange("p (c k) -> p c k", c=8),
                wqh.ap().rearrange("p (c k) -> p c k", c=NHC)
                .rearrange("p c (q k) -> p c q k", q=3)[:, :, qh, :]),
                waits=[t_qs, t_qh, wqs_free, wq_lo], inc=gp)
            stg_free[sb] = wq_lo
        wq_ready = wq_lo
        t_ohA = p.add("gpsimd", lambda gg, P=P: gg.dma_start(
            out=woAh[:], in_=wout_d[P * 128:P * 128 + 64, :]),
            waits=[wo_free] + barA, inc=s_wo[0], inck=16)
        t_ohB = p.add("gpsimd", lambda gg, P=P: gg.dma_start(
            out=woBh[:], in_=wout_d[P * 128 + 64:P * 128 + 128, :]),
            waits=[wo_free] + barA, inc=s_wo[1], inck=16)
        wo_ready = NONE
        for wv in range(4):
            half = wv // 2
            sb = wv % 2
            wsrc = wout_d[P * 128 + half * 64:P * 128 + half * 64 + 64,
                          (wv % 2) * 512:(wv % 2) * 512 + 512]
            t_ws = p.add("sync", lambda sy, wsrc=wsrc, sb=sb: sy.dma_start(
                out=stg2[sb][0:64, 0:512], in_=wsrc),
                waits=[stg_free[sb]] + barA, inc=s_st2[sb], inck=16)
            dst = (woAl if half == 0 else woBl)
            hsrc = (woAh if half == 0 else woBh)
            wo_ready = p.add("gpsimd", lambda gg, dst=dst, hsrc=hsrc, wv=wv, sb=sb: gg.tensor_sub(
                dst[:, (wv % 2) * 512:(wv % 2) * 512 + 512], stg2[sb][0:64, 0:512],
                hsrc[:, (wv % 2) * 512:(wv % 2) * 512 + 512]),
                waits=[t_ws, (t_ohA if half == 0 else t_ohB), wo_free, wo_ready], inc=gp)
            stg_free[sb] = wo_ready

        slA = float(SLOPES[2 * P])
        slB = float(SLOPES[2 * P + 1])
        for hx, (Qxh, Qxl, bt, sl) in enumerate(((QeAh, QeAl, btA, slA),
                                                 (QeBh, QeBl, btB, slB))):
            t_bt = p.add("vector", lambda v, bt=bt, sl=sl: v.tensor_scalar(
                bt[:], dcol[:], 128.0 * sl, -15.0 * 128.0 * sl + EXP_SHIFT, ALU.mult, ALU.add),
                waits=[t_kmi, bt_free[hx]], inc=ve)
            bt_ready[hx] = t_bt
            t_qs1 = p.add("vector", lambda v, hx=hx, sl=sl: v.memset(qsv[hx][64:66, 0:1], -sl),
                          waits=[slope_rdy[hx]], inc=ve)
            t_qs2 = p.add("vector", lambda v, hx=hx, sl=sl: v.memset(qsv[hx][64:65, 0:1], sl),
                          waits=[t_qs1], inc=ve)

            def extq(v, Qxh=Qxh, hx=hx):
                ins = None
                for qb5 in range(S // 512):
                    ins = v.tensor_scalar(Qxh[64:66, qb5 * 512:(qb5 + 1) * 512],
                                          qpat[64:66, 0:512], qsv[hx][64:66, 0:1], None, ALU.mult)
                return ins
            t_eh = p.add("vector", extq, waits=[t_qs2, attn_rows_free, t_kmi], inc=ve)

            def extql(v, Qxh=Qxh, Qxl=Qxl, hx=hx):
                ins = None
                for qb5 in range(S // 512):
                    ins = v.scalar_tensor_tensor(Qxl[64:66, qb5 * 512:(qb5 + 1) * 512],
                                                 qpat[64:66, 0:512], qsv[hx][64:66, 0:1],
                                                 Qxh[64:66, qb5 * 512:(qb5 + 1) * 512],
                                                 ALU.mult, ALU.subtract)
                return ins
            slope_rdy[hx] = p.add("vector", extql, waits=[t_eh], inc=ve)

        qk_ready = NONE
        for tb in range(S // 512):
            def mmq(t, tb=tb):
                ins = None
                for c in range(NHC):
                    ins = mm3(t, psA[0][:, 0:512],
                              wqh[:, c * 192:c * 192 + 128], wql[:, c * 192:c * 192 + 128],
                              xnh[:, c * S + tb * 512:c * S + tb * 512 + 512],
                              xnl[:, c * S + tb * 512:c * S + tb * 512 + 512],
                              c == 0, c == NHC - 1)
                return ins
            t_q = p.add("tensor", mmq, waits=[wq_ready, psA01_free[0], xn_ready], inc=te)

            def mmk(t, tb=tb):
                ins = None
                for c in range(NHC):
                    ins = mm3(t, psA[1][0:64, 0:512],
                              wqh[:, c * 192 + 128:c * 192 + 192], wql[:, c * 192 + 128:c * 192 + 192],
                              xnh[:, c * S + tb * 512:c * S + tb * 512 + 512],
                              xnl[:, c * S + tb * 512:c * S + tb * 512 + 512],
                              c == 0, c == NHC - 1)
                return ins
            t_k = p.add("tensor", mmk, waits=[psA01_free[1]], inc=te)
            wqs_free = t_k

            def evqh(v, tb=tb):
                v.tensor_scalar(QeAh[0:64, tb * 512:(tb + 1) * 512], psA[0][0:64, 0:512], 0.125, None, ALU.mult)
                return v.tensor_scalar(QeBh[0:64, tb * 512:(tb + 1) * 512], psA[0][64:128, 0:512], 0.125, None, ALU.mult)
            t_eqh = p.add("vector", evqh, waits=[t_q, attn_rows_free], inc=ve)

            def evql(v, tb=tb):
                v.scalar_tensor_tensor(QeAl[0:64, tb * 512:(tb + 1) * 512], psA[0][0:64, 0:512],
                                       0.125, QeAh[0:64, tb * 512:(tb + 1) * 512], ALU.mult, ALU.subtract)
                return v.scalar_tensor_tensor(QeBl[0:64, tb * 512:(tb + 1) * 512], psA[0][64:128, 0:512],
                                              0.125, QeBh[0:64, tb * 512:(tb + 1) * 512], ALU.mult, ALU.subtract)
            psA01_free[0] = p.add("vector", evql, waits=[t_eqh], inc=ve)

            t_ekh = p.add("vector", lambda v, tb=tb: v.tensor_copy(
                Keh[0:64, tb * 512:(tb + 1) * 512], psA[1][0:64, 0:512]),
                waits=[t_k, attn_rows_free], inc=ve)
            psA01_free[1] = p.add("vector", lambda v, tb=tb: v.scalar_tensor_tensor(
                Kel[0:64, tb * 512:(tb + 1) * 512], psA[1][0:64, 0:512], 1.0,
                Keh[0:64, tb * 512:(tb + 1) * 512], ALU.mult, ALU.subtract),
                waits=[t_ekh], inc=ve)
            qk_ready = psA01_free[1]
        qk_ready2 = psA01_free[0]

        last_st = NONE
        for qb in range(NQB):
            kcmax = 4 * qb + 4
            for hx in range(2):
                Qeh_, Qel_ = (QeAh, QeAl) if hx == 0 else (QeBh, QeBl)
                btx = btA if hx == 0 else btB
                hh = P % 2
                pv_pend = None
                for kc in range(kcmax):
                    bst = kc % 2

                    def mmst(t, kc=kc, qb=qb, Qeh_=Qeh_, Qel_=Qel_, bst=bst):
                        return mm3(t, psA[bst][:, 0:512],
                                   Keh[0:66, kc * 128:(kc + 1) * 128],
                                   Kel[0:66, kc * 128:(kc + 1) * 128],
                                   Qeh_[0:66, qb * 512:(qb + 1) * 512],
                                   Qel_[0:66, qb * 512:(qb + 1) * 512],
                                   True, True)
                    t_st = p.add("tensor", mmst,
                                 waits=[qk_ready, qk_ready2, psA01_free[bst],
                                        slope_rdy[hx], t_ki],
                                 inc=te)
                    last_st = t_st

                    j = kc - 4 * qb
                    st_rdy = t_st
                    if j >= 0:
                        def premask(v, j=j, bst=bst):
                            if j > 0:
                                v.tensor_scalar(psA[bst][:, 0:128 * j], psA[bst][:, 0:128 * j],
                                                -1e30, None, ALU.add)
                            return v.tensor_tensor(psA[bst][:, 128 * j:128 * j + 128],
                                                   psA[bst][:, 128 * j:128 * j + 128],
                                                   trim[:], ALU.add)
                        st_rdy = p.add("vector", premask, waits=[t_st, t_msk], inc=ve)
                    dd = kc - 4 * qb + 15
                    c_exp = p.add("scalar", lambda s, bst=bst, kc=kc, btx=btx, dd=dd: s.activation(
                        PT[kc % 2][:], psA[bst][:, 0:512], AF.Exp, bias=btx[:, dd:dd + 1], scale=1.0),
                        waits=[st_rdy, PT_free[kc % 2], t_bexp, bt_ready[hx]], inc=sc)
                    bt_free[hx] = c_exp
                    psA01_free[bst] = c_exp

                    def mk_pv(kc, hh, first, last, pt_ready):
                        def mmpv(t, kc=kc, hh=hh, first=first, last=last):
                            return t.matmul(psA[2][0:65, 0:512],
                                            Vp[:, kc * 130 + hh * 65:kc * 130 + hh * 65 + 65],
                                            PT[kc % 2][:],
                                            start=first, stop=last)
                        def emit():
                            t_pv = p.add("tensor", mmpv,
                                         waits=[pt_ready, vp_ready[kc],
                                                (pv_psA2_wait[0] if first else NONE)],
                                         inc=te)
                            PT_free[kc % 2] = t_pv
                            return t_pv
                        return emit
                    if pv_pend is not None:
                        pv_pend()
                    pv_pend = mk_pv(kc, hh, kc == 0, kc == kcmax - 1, c_exp)
                last_pv = pv_pend()
                t_r = p.add("vector", lambda v: v.reciprocal(lb2[64:65, 0:512], psA[2][64:65, 0:512]),
                            waits=[PT_free[(kcmax - 1) % 2], lrow_free], inc=ve)
                t_lb = p.add("tensor", lambda t: t.matmul(psA[3][0:64, 0:512], onesP[64:65, 0:64],
                                                          lb2[64:65, 0:512], start=True, stop=True),
                             waits=[t_r, psA3_free, t_ones], inc=te)
                lrow_free = t_lb
                t_lbe = p.add("vector", lambda v: v.tensor_copy(lb2[0:64, 0:512], psA[3][0:64, 0:512]),
                              waits=[t_lb, qpat_done], inc=ve)
                psA3_free = t_lbe

                def nrm(v, hx=hx, qb=qb):
                    return v.scalar_tensor_tensor(aftmp[:], psA[2][0:64, 0:512], 1.0,
                                                  lb2[0:64, 0:512], ALU.mult, ALU.mult)
                t_nrm = p.add("vector", nrm, waits=[t_lbe, aftmp_free], inc=ve)
                psA2_free = t_nrm
                pv_psA2_wait[0] = t_nrm
                t_afh = p.add("vector", lambda v, hx=hx, qb=qb: v.tensor_copy(
                    afh2[qb % 2][hx][:], aftmp[:]), waits=[t_nrm, afq_free[qb % 2][hx]], inc=ve)
                aftmp_free = p.add("vector", lambda v, hx=hx, qb=qb: v.scalar_tensor_tensor(
                    afl2[qb % 2][hx][:], aftmp[:], 1.0, afh2[qb % 2][hx][:], ALU.mult, ALU.subtract),
                    waits=[t_afh], inc=ve)
                afq_ready_q[qb % 2] = aftmp_free

            def mk_outproj(qb, P=P):
                def emit():
                    for tcl in range(4):
                        tc = qb * 4 + tcl
                        bb = tcl % 2

                        def mmo(t, tcl=tcl, bb=bb, qb=qb):
                            ins = None
                            for nh in range(2):
                                mm3(t, psB[bb][:, nh * 512:(nh + 1) * 512],
                                    afh2[qb % 2][0][:, tcl * 128:(tcl + 1) * 128],
                                    afl2[qb % 2][0][:, tcl * 128:(tcl + 1) * 128],
                                    woAh[:, nh * 512:(nh + 1) * 512],
                                    woAl[:, nh * 512:(nh + 1) * 512], True, False)
                                ins = mm3(t, psB[bb][:, nh * 512:(nh + 1) * 512],
                                          afh2[qb % 2][1][:, tcl * 128:(tcl + 1) * 128],
                                          afl2[qb % 2][1][:, tcl * 128:(tcl + 1) * 128],
                                          woBh[:, nh * 512:(nh + 1) * 512],
                                          woBl[:, nh * 512:(nh + 1) * 512], False, True)
                            return ins
                        t_o = p.add("tensor", mmo, waits=[afq_ready_q[qb % 2], wo_ready, psB_free[bb]], inc=te)
                        if tcl == 3:
                            afq_free[qb % 2] = [t_o, t_o]
                            op_last[0] = t_o

                        def evo(v, tc=tc, bb=bb):
                            return v.tensor_tensor(oa[:, tc * H:(tc + 1) * H], psB[bb][:],
                                                   oa[:, tc * H:(tc + 1) * H], ALU.add)
                        psB_free[bb] = p.add("vector", evo, waits=[t_o, oa_last[tc]], inc=ve)
                        oa_last[tc] = psB_free[bb]
                return emit
            if op_pend[0] is not None:
                op_pend[0]()
            op_pend[0] = mk_outproj(qb)
            if qb == NQB - 1:
                op_pend[0]()
                op_pend[0] = None
                wo_free = op_last[0]
        attn_rows_free = last_st
    es_b.close()

    barB = barrier()
    es_c = ExitStack()
    wgh = es_c.enter_context(nc.sbuf_tensor("wgh", [128, NHC * IEXP], BF16))
    wgl = es_c.enter_context(nc.sbuf_tensor("wgl", [128, NHC * IEXP], BF16))
    wuh = es_c.enter_context(nc.sbuf_tensor("wuh", [128, NHC * IEXP], BF16))
    wul = es_c.enter_context(nc.sbuf_tensor("wul", [128, NHC * IEXP], BF16))
    wdh = es_c.enter_context(nc.sbuf_tensor("wdh", [128, 4 * H], BF16))
    wdl = es_c.enter_context(nc.sbuf_tensor("wdl", [128, 4 * H], BF16))
    acth = [es_c.enter_context(nc.sbuf_tensor("acth%d" % i, [128, 4 * 512], BF16)) for i in range(2)]
    actl = [es_c.enter_context(nc.sbuf_tensor("actl%d" % i, [128, 4 * 512], BF16)) for i in range(2)]
    tact = es_c.enter_context(nc.sbuf_tensor("tact", [128, 512], F32))
    sg_t = [es_c.enter_context(nc.sbuf_tensor("sg%d" % i, [128, 512], F32)) for i in range(2)]

    wg_free = NONE
    wu_free = NONE
    wd_free = NONE
    g_ps_free = [psA01_free[0], psA01_free[1]]
    u_ps_free = [psA2_free, psA3_free]
    d_ps_free = [psB_free[0], psB_free[1]]
    act_free = [NONE, NONE]
    sg_free = [NONE, NONE]
    tact_free = NONE
    pend_wd = [None]

    elist = list(exp_ids) if exp_ids is not None else list(range(n_exp))

    wready = {}

    def emit_w(idx):
        def one(src_d, htile, ltile, s_hi, s_st_unused, mfree, C, W, e):
            nonlocal_stg = None
            t_hi = p.add("gpsimd", lambda gg, e=e, src_d=src_d, htile=htile, C=C: gg.dma_start(
                out=htile.ap().rearrange("p (c i) -> p c i", c=C),
                in_=src_d[e].rearrange("(c p) i -> p c i", p=128)),
                waits=[mfree] + barB, inc=s_hi, inck=16)
            lo_done = NONE
            cpq = 512 // W if W <= 512 else 0
            for hf in range(8):
                sb = hf % 2
                if W == 512:
                    src_ap = src_d[e].rearrange("(c p) i -> p c i", p=128)[:, hf:hf + 1, :]
                    dst_ap = stg2[sb].ap().rearrange("p (c i) -> p c i", c=1)
                else:
                    src_ap = src_d[e].rearrange("(c p) (q i) -> p c q i", p=128, q=2)[:, hf // 2, hf % 2, :]
                    dst_ap = stg2[sb][:]
                t_st = p.add("sync", lambda sy, src_ap=src_ap, dst_ap=dst_ap: sy.dma_start(
                    out=dst_ap, in_=src_ap),
                    waits=[stg_free[sb]] + barB, inc=s_st2[sb], inck=16)
                lo_done = p.add("gpsimd", lambda gg, htile=htile, ltile=ltile, hf=hf, sb=sb: gg.tensor_sub(
                    ltile[:, hf * 512:(hf + 1) * 512], stg2[sb][:],
                    htile[:, hf * 512:(hf + 1) * 512]),
                    waits=[t_st, t_hi, mfree, lo_done], inc=gp)
                stg_free[sb] = lo_done
            return lo_done
        e = elist[idx]
        return {
            "g": lambda mfree: one(eg_d, wgh, wgl, s_gh, s_wg, mfree, 8, 512, e),
            "u": lambda mfree: one(eu_d, wuh, wul, s_uh, s_wu, mfree, 8, 512, e),
            "d": lambda mfree: one(ed_d, wdh, wdl, s_dh, s_wd, mfree, 4, 1024, e),
        }

    w0 = emit_w(0)
    wready["g"] = w0["g"](NONE)
    wready["u"] = w0["u"](NONE)
    wready["d"] = w0["d"](NONE)

    down_pend = [None]

    def mk_down(e, tbp, ab, act_tok, wd_tok):
        def emit():
            for tcl in range(4):
                tc = tbp * 4 + tcl
                db = tc % 2

                def mmd(t, tcl=tcl, ab=ab, db=db):
                    ins = None
                    for nh in range(2):
                        for ic in range(4):
                            ins = mm3(t, psB[db][:, nh * 512:(nh + 1) * 512],
                                      acth[ab][:, ic * 512 + tcl * 128:ic * 512 + tcl * 128 + 128],
                                      actl[ab][:, ic * 512 + tcl * 128:ic * 512 + tcl * 128 + 128],
                                      wdh[:, ic * H + nh * 512:ic * H + nh * 512 + 512],
                                      wdl[:, ic * H + nh * 512:ic * H + nh * 512 + 512],
                                      ic == 0, ic == 3)
                    return ins
                t_d = p.add("tensor", mmd, waits=[act_tok, wd_tok, d_ps_free[db]], inc=te)
                if tcl == 3:
                    act_free[ab] = t_d
                    down_last[0] = t_d

                if e < NE:
                    def evd(v, tc=tc, db=db, e=e):
                        return v.scalar_tensor_tensor(oa[:, tc * H:(tc + 1) * H], psB[db][:],
                                                      wde[:, tc * 32 + e:tc * 32 + e + 1],
                                                      oa[:, tc * H:(tc + 1) * H], ALU.mult, ALU.add)
                    d_ps_free[db] = p.add("vector", evd, waits=[t_d, oa_last[tc], wde_ready], inc=ve)
                else:
                    def evd(v, tc=tc, db=db):
                        return v.tensor_tensor(oa[:, tc * H:(tc + 1) * H], psB[db][:],
                                               oa[:, tc * H:(tc + 1) * H], ALU.add)
                    d_ps_free[db] = p.add("vector", evd, waits=[t_d, oa_last[tc]], inc=ve)
                oa_last[tc] = d_ps_free[db]
        return emit

    down_last = [NONE]
    for idx, e in enumerate(elist):
        wg_now, wu_now, wd_now = wready["g"], wready["u"], wready["d"]
        for tbp in range(NTBP):
            ab = tbp % 2
            act_ready = NONE
            for ic in range(4):
                ib = ic % 2

                def mmgp(t, tbp=tbp, ic=ic, ib=ib):
                    ins = None
                    for c in range(NHC):
                        ins = mm3(t, psA[ib][:, 0:512],
                                  wgh[:, c * IEXP + ic * 128:c * IEXP + ic * 128 + 128],
                                  wgl[:, c * IEXP + ic * 128:c * IEXP + ic * 128 + 128],
                                  xnh[:, c * S + tbp * 512:c * S + tbp * 512 + 512],
                                  xnl[:, c * S + tbp * 512:c * S + tbp * 512 + 512],
                                  c == 0, c == NHC - 1)
                    return ins
                t_g = p.add("tensor", mmgp, waits=[wg_now, g_ps_free[ib], xn_ready], inc=te)

                def mmup(t, tbp=tbp, ic=ic, ib=ib):
                    ins = None
                    for c in range(NHC):
                        ins = mm3(t, psA[2 + ib][:, 0:512],
                                  wuh[:, c * IEXP + ic * 128:c * IEXP + ic * 128 + 128],
                                  wul[:, c * IEXP + ic * 128:c * IEXP + ic * 128 + 128],
                                  xnh[:, c * S + tbp * 512:c * S + tbp * 512 + 512],
                                  xnl[:, c * S + tbp * 512:c * S + tbp * 512 + 512],
                                  c == 0, c == NHC - 1)
                    return ins
                t_u = p.add("tensor", mmup, waits=[wu_now, u_ps_free[ib]], inc=te)

                if ic == 0 and down_pend[0] is not None:
                    down_pend[0]()
                    down_pend[0] = None
                    if wready.get("d") is None and pend_wd[0] is not None:
                        wready["d"] = pend_wd[0][0]["d"](down_last[0])
                        wd_now = wready["d"]
                        pend_wd[0] = None

                c_si = p.add("scalar", lambda s, ib=ib: s.activation(sg_t[ib][:], psA[ib][:, 0:512], AF.Silu),
                             waits=[t_g, sg_free[ib]], inc=sc)
                g_ps_free[ib] = c_si

                v_t1 = p.add("vector", lambda v, ib=ib: v.tensor_mul(tact[:], sg_t[ib][:], psA[2 + ib][:, 0:512]),
                             waits=[c_si, t_u, tact_free], inc=ve)
                u_ps_free[ib] = v_t1
                sg_free[ib] = v_t1
                v_ah = p.add("vector", lambda v, ab=ab, ic=ic: v.tensor_copy(
                    acth[ab][:, ic * 512:(ic + 1) * 512], tact[:]),
                    waits=[v_t1, (act_free[ab] if ic == 0 else NONE)], inc=ve)
                v_al = p.add("vector", lambda v, ab=ab, ic=ic: v.scalar_tensor_tensor(
                    actl[ab][:, ic * 512:(ic + 1) * 512], tact[:], 1.0,
                    acth[ab][:, ic * 512:(ic + 1) * 512], ALU.mult, ALU.subtract),
                    waits=[v_ah], inc=ve)
                tact_free = v_al
                act_ready = v_al

                if idx + 1 < len(elist) and tbp == NTBP - 1 and ic == 3:
                    wn = emit_w(idx + 1)
                    next_g = wn["g"](t_g)
                    next_u = wn["u"](t_u)
                    pend_wd[0] = (wn, None)

            down_pend[0] = mk_down(e, tbp, ab, act_ready, wd_now)
        if idx + 1 < len(elist):
            wready = {"g": next_g, "u": next_u, "d": None}
    if down_pend[0] is not None:
        down_pend[0]()
        down_pend[0] = None

    es_c.close()
    barC = barrier()
    xt4 = [es.enter_context(nc.sbuf_tensor("xt4%d" % i, [128, H], F32)) for i in range(2)]
    res_t = [es.enter_context(nc.sbuf_tensor("res%d" % i, [128, H], F32)) for i in range(2)]
    res_free = [NONE, NONE]
    x2_free = [NONE, NONE]
    for tc in range(NTC):
        b = tc % 2
        t_x = p.add("sync", lambda sy, tc=tc, b=b: sy.dma_start(
            out=xt4[b][:], in_=x_d[tc * 128:(tc + 1) * 128, :]),
            waits=[x2_free[b]] + barC, inc=s_x[b], inck=16)

        def radd(v, tc=tc, b=b):
            return v.tensor_tensor(res_t[b][:], oa[:, tc * H:(tc + 1) * H], xt4[b][:], ALU.add)
        t_r = p.add("vector", radd, waits=[t_x, oa_last[tc], res_free[b]], inc=ve)
        x2_free[b] = t_r

        res_free[b] = p.add("sync", lambda sy, tc=tc, b=b: sy.dma_start(
            out=out_d[tc * 128:(tc + 1) * 128, :], in_=res_t[b][:]),
            waits=[t_r], inc=s_o[b], inck=16)
    p.add("sync", lambda sy: sy.nop(), waits=[(s_o[0], s_o[0].n), (s_o[1], s_o[1].n)])

    blk = es.enter_context(nc.Block())
    p.emit(blk)
    es.close()
    return nc


def _prep_weights(W_qkv, W_out, gate_w, gate_bias, eg, eu, edn, sg, su, sd):
    cols = []
    for g in range(4):
        for k in range(4 * g, 4 * g + 4):
            cols.extend(range(3072 + 64 * k, 3072 + 64 * k + 64))
        for Pp in range(4 * g, 4 * g + 4):
            cols.extend(range(64 * (2 * Pp), 64 * (2 * Pp) + 64))
            cols.extend(range(64 * (2 * Pp + 1), 64 * (2 * Pp + 1) + 64))
            cols.extend(range(2048 + 64 * Pp, 2048 + 64 * Pp + 64))
    wqkv_re = np.ascontiguousarray(W_qkv[:, cols])
    wout = np.ascontiguousarray(W_out)
    gwT = np.ascontiguousarray(gate_w.T)
    gb = np.ascontiguousarray(gate_bias.reshape(1, 32))
    sgT = sg.T
    suT = su.T
    sdT = sd.T
    eg_all = np.ascontiguousarray(np.concatenate(
        [eg.transpose(0, 2, 1), sgT[None, :, 0:512], sgT[None, :, 512:1024]], axis=0))
    eu_all = np.ascontiguousarray(np.concatenate(
        [eu.transpose(0, 2, 1), suT[None, :, 0:512], suT[None, :, 512:1024]], axis=0))
    ed_all = np.ascontiguousarray(np.concatenate(
        [edn.transpose(0, 2, 1), sdT[None, 0:512, :], sdT[None, 512:1024, :]], axis=0))
    return wqkv_re, wout, gwT, gb, eg_all, eu_all, ed_all


def kernel(L0_attn_inputs, L0_attn_W_qkv, L0_attn_W_out, L0_ffn_gate_w,
           L0_ffn_gate_bias, L0_ffn_expert_gate, L0_ffn_expert_up,
           L0_ffn_expert_down, L0_ffn_shared_gate, L0_ffn_shared_up,
           L0_ffn_shared_down):
    x = np.ascontiguousarray(np.asarray(L0_attn_inputs, dtype=np.float32))
    wqkv_re, wout, gwT, gb, eg_all, eu_all, ed_all = _prep_weights(
        np.asarray(L0_attn_W_qkv, np.float32), np.asarray(L0_attn_W_out, np.float32),
        np.asarray(L0_ffn_gate_w, np.float32), np.asarray(L0_ffn_gate_bias, np.float32),
        np.asarray(L0_ffn_expert_gate, np.float32), np.asarray(L0_ffn_expert_up, np.float32),
        np.asarray(L0_ffn_expert_down, np.float32), np.asarray(L0_ffn_shared_gate, np.float32),
        np.asarray(L0_ffn_shared_up, np.float32), np.asarray(L0_ffn_shared_down, np.float32))

    nc = build_program()
    in_maps = []
    for b in range(8):
        in_maps.append({
            "x": x[b], "wqkv": wqkv_re, "wout": wout, "gwT": gwT, "gb": gb,
            "eg": eg_all, "eu": eu_all, "ed": ed_all,
        })
    res = run_bass_kernel_spmd(nc, in_maps, core_ids=list(range(8)))
    return np.stack([r["out"] for r in res.results], axis=0)


# version 07
# speedup vs baseline: 1.1083x; 1.1083x over previous
import sys
import math
import numpy as np

try:
    import concourse.bass as bass
except ImportError:
    sys.path.insert(0, "/opt/trn_rl_repo")
    import concourse.bass as bass
import concourse.mybir as mybir
from concourse.bass_utils import run_bass_kernel_spmd
from contextlib import ExitStack

F32 = mybir.dt.float32
BF16 = mybir.dt.bfloat16
AF = mybir.ActivationFunctionType
ALU = mybir.AluOpType
AX = mybir.AxisListType

H = 1024
D = 64
NE = 32
IEXP = 512
TOPK = 8
NGRP = 4
RSCAL = 2.5
NHC = H // 128
EXP_SHIFT = -16.0


def _alibi_slopes(num_heads):
    closest_pow2 = 2 ** int(math.floor(math.log2(num_heads)))
    base = 2.0 ** (-(2.0 ** (-(math.log2(closest_pow2) - 3))))
    powers = np.arange(1, closest_pow2 + 1, dtype=np.float32)
    slopes = base ** powers
    if closest_pow2 != num_heads:
        extra_base = 2.0 ** (-(2.0 ** (-(math.log2(2 * closest_pow2) - 3))))
        extra_powers = np.arange(1, 2 * (num_heads - closest_pow2) + 1, 2, dtype=np.float32)
        slopes = np.concatenate([slopes, extra_base ** extra_powers])
    return slopes[:num_heads].astype(np.float32)

SLOPES = _alibi_slopes(32)

ENGINES = ("sync", "vector", "scalar", "tensor", "gpsimd")
NONE = None


class Sem:
    def __init__(self, es, nc, name):
        self.h = es.enter_context(nc.semaphore(name))
        self.n = 0
        self.name = name


class Plan:
    def __init__(self):
        self.ops = []

    def add(self, eng, fn, waits=(), inc=None, inck=1):
        ws = [w for w in waits if w is not None and w[1] > 0]
        incs = []
        tok = None
        if inc is not None:
            inc.n += inck
            incs.append((inc, inck))
            tok = (inc, inc.n)
        self.ops.append((eng, fn, ws, incs))
        return tok

    def emit(self, blk):
        def mk(name):
            def f(eng):
                waited = {}
                for (e, fn, ws, incs) in self.ops:
                    if e != name:
                        continue
                    for (s, v) in ws:
                        if waited.get(s.name, 0) >= v:
                            continue
                        eng.wait_ge(s.h, v)
                        waited[s.name] = v
                    last = fn(eng)
                    for (s, k) in incs:
                        last.then_inc(s.h, k)
            return f
        for name in ENGINES:
            getattr(blk, name)(mk(name))


def build_program(S=2048, n_exp=34, n_pack=16, exp_ids=None):
    NTC = S // 128
    NQB = S // 512
    NTBP = S // 512
    nc = bass.Bass("TRN2", target_bir_lowering=False)
    es = ExitStack()

    x_d = nc.dram_tensor("x", [S, H], F32, kind="ExternalInput").ap()
    wqkv_d = nc.dram_tensor("wqkv", [H, 4096], F32, kind="ExternalInput").ap()
    wout_d = nc.dram_tensor("wout", [2048, H], F32, kind="ExternalInput").ap()
    gwT_d = nc.dram_tensor("gwT", [H, 32], F32, kind="ExternalInput").ap()
    gb_d = nc.dram_tensor("gb", [1, 32], F32, kind="ExternalInput").ap()
    eg_d = nc.dram_tensor("eg", [34, H, IEXP], F32, kind="ExternalInput").ap()
    eu_d = nc.dram_tensor("eu", [34, H, IEXP], F32, kind="ExternalInput").ap()
    ed_d = nc.dram_tensor("ed", [34, IEXP, H], F32, kind="ExternalInput").ap()
    out_d = nc.dram_tensor("out", [S, H], F32, kind="ExternalOutput").ap()

    xnh = es.enter_context(nc.sbuf_tensor("xnh", [128, NHC * S], BF16))
    xnl = es.enter_context(nc.sbuf_tensor("xnl", [128, NHC * S], BF16))
    oa = es.enter_context(nc.sbuf_tensor("oa", [128, NTC * H], F32))
    wde = es.enter_context(nc.sbuf_tensor("wde", [128, NTC * 32], F32))
    onesP = es.enter_context(nc.sbuf_tensor("onesP", [128, 128], F32))
    stg2 = [es.enter_context(nc.sbuf_tensor("stg2%d" % i, [128, 512], F32)) for i in range(2)]
    bexp = es.enter_context(nc.sbuf_tensor("bexp", [128, 1], F32))

    psA = [es.enter_context(nc.psum_tensor("psA%d" % i, [128, 512], F32)) for i in range(4)]
    psB = [es.enter_context(nc.psum_tensor("psB%d" % i, [128, 1024], F32)) for i in range(2)]

    s_x = [Sem(es, nc, "s_x0"), Sem(es, nc, "s_x1")]
    s_gw = Sem(es, nc, "s_gw")
    s_vg = Sem(es, nc, "s_vg")
    s_wq = Sem(es, nc, "s_wq")
    s_wo = [Sem(es, nc, "s_wo0"), Sem(es, nc, "s_wo1")]
    s_wg = Sem(es, nc, "s_wg")
    s_wu = Sem(es, nc, "s_wu")
    s_wd = Sem(es, nc, "s_wd")
    s_gh = Sem(es, nc, "s_gh")
    s_qh = Sem(es, nc, "s_qh")
    s_vh = Sem(es, nc, "s_vh")
    s_uh = Sem(es, nc, "s_uh")
    s_dh = Sem(es, nc, "s_dh")
    s_o = [Sem(es, nc, "s_o0"), Sem(es, nc, "s_o1")]
    s_st2 = [Sem(es, nc, "s_st20"), Sem(es, nc, "s_st21")]
    ve = Sem(es, nc, "ve")
    sc = Sem(es, nc, "sc")
    te = Sem(es, nc, "te")
    gp = Sem(es, nc, "gp")

    p = Plan()

    def mm3(t, out_ap, wh, wl, xh, xl, first, last):
        t.matmul(out_ap, wh, xh, start=first, stop=False)
        t.matmul(out_ap, wh, xl, start=False, stop=False)
        return t.matmul(out_ap, wl, xh, start=False, stop=last)


    t_bexp = p.add("vector", lambda v: v.memset(bexp[:], EXP_SHIFT), inc=ve)
    t_oaz = p.add("vector", lambda v: v.memset(oa[:], 0.0), inc=ve)
    t_ones = p.add("vector", lambda v: v.memset(onesP[:], 1.0), inc=ve)

    es_a = ExitStack()
    xt = [es_a.enter_context(nc.sbuf_tensor("xt%d" % i, [128, H], F32)) for i in range(2)]
    lt2 = [es_a.enter_context(nc.sbuf_tensor("lt%d" % i, [128, H], F32)) for i in range(2)]
    sqt2 = [es_a.enter_context(nc.sbuf_tensor("sqt%d" % i, [128, H], F32)) for i in range(2)]
    lns = es_a.enter_context(nc.sbuf_tensor("lns", [128, 16], F32))
    xnt = [es_a.enter_context(nc.sbuf_tensor("xnt%d" % i, [128, H], F32)) for i in range(2)]
    itmp = es_a.enter_context(nc.sbuf_tensor("itmp", [128, 128], F32))
    ident = es_a.enter_context(nc.sbuf_tensor("ident", [128, 128], F32))

    t_ii = p.add("gpsimd", lambda g: g.iota(itmp[:], pattern=[[1, 128]], base=127,
                                            channel_multiplier=-1,
                                            allow_small_or_imprecise_dtypes=True), inc=gp)
    t_id = p.add("vector", lambda v: v.tensor_scalar(ident[:], itmp[:], 127.0, None, ALU.is_equal),
                 waits=[t_ii], inc=ve)

    gwTs = es_a.enter_context(nc.sbuf_tensor("gwTs", [128, NHC * 32], F32))
    gwh = es_a.enter_context(nc.sbuf_tensor("gwh", [128, NHC * 32], BF16))
    gwl = es_a.enter_context(nc.sbuf_tensor("gwl", [128, NHC * 32], BF16))
    gbr = es_a.enter_context(nc.sbuf_tensor("gbr", [1, 32], F32))
    gbb = es_a.enter_context(nc.sbuf_tensor("gbb", [128, 32], F32))
    gsc = es_a.enter_context(nc.sbuf_tensor("gsc", [128, 512], F32))
    sco = es_a.enter_context(nc.sbuf_tensor("sco", [128, NTC * 32], F32))

    t_w1 = p.add("sync", lambda sy: sy.dma_start(
        out=gwTs.ap().rearrange("p (c e) -> p c e", c=NHC),
        in_=gwT_d.rearrange("(c p) e -> p c e", p=128)), inc=s_gw, inck=16)
    t_w2 = p.add("sync", lambda sy: sy.dma_start(out=gbr[:], in_=gb_d), inc=s_gw, inck=16)
    t_w2 = (s_gw, s_gw.n)

    t_gsh = p.add("vector", lambda v: v.tensor_copy(gwh[:], gwTs[:]), waits=[t_w2], inc=ve)
    t_gsl = p.add("vector", lambda v: v.scalar_tensor_tensor(gwl[:], gwTs[:], 1.0, gwh[:],
                                                             ALU.mult, ALU.subtract),
                  waits=[t_gsh], inc=ve)
    t_bb = p.add("tensor", lambda t: t.matmul(psA[3][:, 0:32], onesP[0:1, 0:128], gbr[:], start=True, stop=True),
                 waits=[t_w2, t_ones], inc=te)
    t_gbb = p.add("vector", lambda v: v.tensor_copy(gbb[:], psA[3][:, 0:32]), waits=[t_bb], inc=ve)
    psA3_free0 = t_bb_evac_token = t_gbb


    sig_done = [NONE] * NTC
    psA_gfree = [NONE, NONE]

    def emit_gate(tc, xn_tok):
        B0 = (tc % 2) * 256
        def mmg(t, tc=tc):
            ins = None
            for c in range(NHC):
                ins = mm3(t, psA[tc % 2][:, 0:32],
                          xnh[:, c * S + tc * 128:c * S + tc * 128 + 128],
                          xnl[:, c * S + tc * 128:c * S + tc * 128 + 128],
                          gwh[:, c * 32:(c + 1) * 32], gwl[:, c * 32:(c + 1) * 32],
                          c == 0, c == NHC - 1)
            return ins
        t_g = p.add("tensor", mmg, waits=[xn_tok, t_gsl, psA_gfree[tc % 2]], inc=te)
        c_sig = p.add("scalar", lambda s, tc=tc: s.activation(
            sco[:, tc * 32:(tc + 1) * 32], psA[tc % 2][:, 0:32], AF.Sigmoid),
            waits=[t_g], inc=sc)
        sig_done[tc] = c_sig
        psA_gfree[tc % 2] = c_sig

        scox = sco[:, tc * 32:(tc + 1) * 32]
        sct = gsc[:, B0 + 0:B0 + 32]
        gs4 = gsc[:, B0 + 32:B0 + 36]
        msk = gsc[:, B0 + 36:B0 + 68]
        msc = gsc[:, B0 + 68:B0 + 100]
        sel = gsc[:, B0 + 100:B0 + 132]
        rmk = gsc[:, B0 + 132:B0 + 164]
        m1 = gsc[:, B0 + 164:B0 + 165]
        t8 = gsc[:, B0 + 165:B0 + 173]
        m2 = gsc[:, B0 + 173:B0 + 174]
        gm = gsc[:, B0 + 174:B0 + 175]
        mk1 = gsc[:, B0 + 175:B0 + 179]
        gq2 = gsc[:, B0 + 179:B0 + 183]
        wr = gsc[:, B0 + 183:B0 + 215]
        dsm = gsc[:, B0 + 215:B0 + 216]
        prev = [NONE]

        def ch(fn, extra=()):
            prev[0] = p.add("vector", fn, waits=[prev[0]] + list(extra), inc=ve)
            return prev[0]

        yield
        ch(lambda v, scox=scox, sct=sct: v.tensor_tensor(sct, scox, gbb[:], ALU.add), extra=[c_sig])
        yield
        for g in range(NGRP):
            sl_ = sct[:, 8 * g:8 * g + 8]
            ch(lambda v, sl_=sl_, m1=m1: v.tensor_reduce(m1, sl_, AX.X, ALU.max))
            yield
            ch(lambda v, sl_=sl_, t8=t8, m1=m1: v.tensor_scalar(t8, sl_, m1, None, ALU.is_equal))
            yield
            ch(lambda v, sl_=sl_, t8=t8: v.scalar_tensor_tensor(t8, t8, -1e30, sl_, ALU.mult, ALU.add))
            yield
            ch(lambda v, t8=t8, m2=m2: v.tensor_reduce(m2, t8, AX.X, ALU.max))
            yield
            ch(lambda v, g=g, gs4=gs4, m1=m1, m2=m2: v.tensor_tensor(gs4[:, g:g + 1], m1, m2, ALU.add))
            yield
        ch(lambda v, gm=gm, gs4=gs4: v.tensor_reduce(gm, gs4, AX.X, ALU.max))
        yield
        ch(lambda v, mk1=mk1, gs4=gs4, gm=gm: v.tensor_scalar(mk1, gs4, gm, None, ALU.is_equal))
        yield
        ch(lambda v, gq2=gq2, mk1=mk1, gs4=gs4: v.scalar_tensor_tensor(gq2, mk1, -1e30, gs4, ALU.mult, ALU.add))
        yield
        ch(lambda v, gm=gm, gq2=gq2: v.tensor_reduce(gm, gq2, AX.X, ALU.max))
        yield
        ch(lambda v, gq2=gq2, gm=gm: v.tensor_scalar(gq2, gq2, gm, None, ALU.is_equal))
        yield
        ch(lambda v, mk1=mk1, gq2=gq2: v.tensor_tensor(mk1, mk1, gq2, ALU.add))
        yield
        ch(lambda v, msk=msk, mk1=mk1: v.tensor_copy(msk.rearrange("p (g o) -> p g o", o=8),
                                                     mk1.rearrange("p (g o) -> p g o", o=1).broadcast_to([128, 4, 8])))
        yield
        ch(lambda v, msc=msc, sct=sct, msk=msk: v.tensor_mul(msc, sct, msk))
        yield
        ch(lambda v, sel=sel: v.memset(sel, 0.0))
        yield
        for r in range(TOPK):
            ch(lambda v, m1=m1, msc=msc: v.tensor_reduce(m1, msc, AX.X, ALU.max))
            yield
            ch(lambda v, rmk=rmk, msc=msc, m1=m1: v.tensor_scalar(rmk, msc, m1, None, ALU.is_equal))
            yield
            ch(lambda v, sel=sel, rmk=rmk: v.tensor_tensor(sel, sel, rmk, ALU.add))
            yield
            ch(lambda v, msc=msc, rmk=rmk: v.scalar_tensor_tensor(msc, rmk, -1e30, msc, ALU.mult, ALU.add))
            yield
        ch(lambda v, wr=wr, scox=scox, sel=sel: v.tensor_mul(wr, scox, sel))
        yield
        ch(lambda v, dsm=dsm, wr=wr: v.tensor_reduce(dsm, wr, AX.X, ALU.add))
        yield
        ch(lambda v, dsm=dsm: v.tensor_scalar(dsm, dsm, 1.0, 1e-20, ALU.mult, ALU.add))
        yield
        ch(lambda v, m1=m1, dsm=dsm: v.reciprocal(m1, dsm))
        yield
        ch(lambda v, m1=m1: v.tensor_scalar(m1, m1, RSCAL, None, ALU.mult))
        yield
        ch(lambda v, tc=tc, wr=wr, m1=m1: v.tensor_scalar(wde[:, tc * 32:(tc + 1) * 32], wr, m1, None, ALU.mult))
        yield

    x_free = [NONE, NONE]
    xnt_free = [NONE, NONE]
    psA3_cell = [NONE]
    xn_tc = [NONE] * NTC

    def ln_chain(tc):
        b = tc % 2
        t_x = p.add("sync", lambda sy, tc=tc, b=b: sy.dma_start(
            out=xt[b][:], in_=x_d[tc * 128:(tc + 1) * 128, :]),
            waits=[x_free[b]], inc=s_x[b], inck=16)
        yield
        lt = lt2[b]
        sqt = sqt2[b]
        L0 = b * 8
        prev = [NONE]

        def ch(fn, extra=()):
            prev[0] = p.add("vector", fn, waits=[prev[0]] + list(extra), inc=ve)
            return prev[0]

        ch(lambda v, b=b: v.tensor_reduce(lns[:, L0:L0 + 1], xt[b][:], AX.X, ALU.add), extra=[t_x])
        yield
        ch(lambda v: v.tensor_scalar(lns[:, L0 + 1:L0 + 2], lns[:, L0:L0 + 1], 1.0 / H, None, ALU.mult))
        yield
        x_free[b] = ch(lambda v, b=b, lt=lt: v.tensor_scalar(lt[:], xt[b][:], lns[:, L0 + 1:L0 + 2], None, ALU.subtract))
        yield
        ch(lambda v, lt=lt, sqt=sqt: v.tensor_mul(sqt[:], lt[:], lt[:]))
        yield
        ch(lambda v, sqt=sqt: v.tensor_reduce(lns[:, L0 + 2:L0 + 3], sqt[:], AX.X, ALU.add))
        yield
        t6 = ch(lambda v: v.tensor_scalar(lns[:, L0 + 3:L0 + 4], lns[:, L0 + 2:L0 + 3], 1.0 / H, 1e-5, ALU.mult, ALU.add))
        yield
        t7 = p.add("scalar", lambda s: s.activation(lns[:, L0 + 4:L0 + 5], lns[:, L0 + 3:L0 + 4], AF.Sqrt),
                   waits=[t6], inc=sc)
        yield
        ch(lambda v: v.reciprocal(lns[:, L0 + 5:L0 + 6], lns[:, L0 + 4:L0 + 5]), extra=[t7])
        yield
        ch(lambda v: v.tensor_mul(lns[:, L0 + 6:L0 + 7], lns[:, L0 + 3:L0 + 4], lns[:, L0 + 5:L0 + 6]))
        yield
        ch(lambda v: v.tensor_tensor(lns[:, L0 + 7:L0 + 8], lns[:, L0 + 4:L0 + 5], lns[:, L0 + 6:L0 + 7], ALU.add))
        yield
        ch(lambda v: v.tensor_scalar(lns[:, L0 + 6:L0 + 7], lns[:, L0 + 7:L0 + 8], 0.5, None, ALU.mult))
        yield
        ch(lambda v: v.reciprocal(lns[:, L0 + 7:L0 + 8], lns[:, L0 + 6:L0 + 7]))
        yield
        t12 = ch(lambda v, b=b, lt=lt: v.tensor_scalar(xnt[b][:], lt[:], lns[:, L0 + 7:L0 + 8], None, ALU.mult),
                 extra=[xnt_free[b]])
        yield
        for c2 in range(NHC // 2):
            def tp(t, b=b, c2=c2):
                t.transpose(psA[3][:, 0:128], xnt[b][:, (2 * c2) * 128:(2 * c2) * 128 + 128], ident[:])
                return t.transpose(psA[3][:, 128:256], xnt[b][:, (2 * c2 + 1) * 128:(2 * c2 + 1) * 128 + 128], ident[:])
            t_tp = p.add("tensor", tp, waits=[t12, psA3_cell[0], t_id], inc=te)
            xnt_free[b] = t_tp

            def evh(v, tc=tc, c2=c2):
                for k2 in range(2):
                    dst = (2 * c2 + k2) * S + tc * 128
                    ins = v.tensor_copy(xnh[:, dst:dst + 128], psA[3][:, k2 * 128:k2 * 128 + 128])
                return ins
            t_evh = p.add("vector", evh, waits=[t_tp], inc=ve)

            def evl(v, tc=tc, c2=c2):
                for k2 in range(2):
                    dst = (2 * c2 + k2) * S + tc * 128
                    ins = v.scalar_tensor_tensor(xnl[:, dst:dst + 128], psA[3][:, k2 * 128:k2 * 128 + 128],
                                                 1.0, xnh[:, dst:dst + 128], ALU.mult, ALU.subtract)
                return ins
            psA3_cell[0] = p.add("vector", evl, waits=[t_evh], inc=ve)
            yield
        xn_tc[tc] = psA3_cell[0]
        for _ in emit_gate(tc, xn_tc[tc]):
            yield

    alive = []
    nxt = [0]

    def refill():
        while len(alive) < 2 and nxt[0] < NTC:
            alive.append(ln_chain(nxt[0]))
            nxt[0] += 1

    refill()
    while alive:
        for g in list(alive):
            try:
                next(g)
            except StopIteration:
                alive.remove(g)
                refill()
    psA3_free0 = psA3_cell[0]
    xn_ready = psA3_free0
    wde_ready = (ve, ve.n)
    psA3_free = psA3_free0



    es_a.close()

    def barrier():
        return [(ve, ve.n), (sc, sc.n), (te, te.n), (gp, gp.n)]

    barA = barrier()

    es_b = ExitStack()
    QeAh = es_b.enter_context(nc.sbuf_tensor("QeAh", [66, S], BF16))
    QeAl = es_b.enter_context(nc.sbuf_tensor("QeAl", [66, S], BF16))
    QeBh = es_b.enter_context(nc.sbuf_tensor("QeBh", [66, S], BF16))
    QeBl = es_b.enter_context(nc.sbuf_tensor("QeBl", [66, S], BF16))
    Keh = es_b.enter_context(nc.sbuf_tensor("Keh", [66, S], BF16))
    Kel = es_b.enter_context(nc.sbuf_tensor("Kel", [66, S], BF16))
    dcol = es_b.enter_context(nc.sbuf_tensor("dcol", [128, 19], F32))
    btA = es_b.enter_context(nc.sbuf_tensor("btA", [128, 19], F32))
    btB = es_b.enter_context(nc.sbuf_tensor("btB", [128, 19], F32))
    aftmp = es_b.enter_context(nc.sbuf_tensor("aftmp", [64, 512], F32))
    qpat = es_b.enter_context(nc.sbuf_tensor("qpat", [128, 512], F32))
    trim = es_b.enter_context(nc.sbuf_tensor("trim", [128, 128], F32))
    itm2 = es_b.enter_context(nc.sbuf_tensor("itm2", [128, 128], F32))
    qsv = [es_b.enter_context(nc.sbuf_tensor("qsv%d" % i, [128, 1], F32)) for i in range(2)]
    Vp = es_b.enter_context(nc.sbuf_tensor("Vp", [128, NTC * 130], F32))
    vgh = es_b.enter_context(nc.sbuf_tensor("vgh", [128, NHC * 128], BF16))
    vgl = es_b.enter_context(nc.sbuf_tensor("vgl", [128, NHC * 128], BF16))
    wqh = es_b.enter_context(nc.sbuf_tensor("wqh", [128, 8 * 192], BF16))
    wql = es_b.enter_context(nc.sbuf_tensor("wql", [128, 8 * 192], BF16))
    PT = [es_b.enter_context(nc.sbuf_tensor("PT%d" % i, [128, 512], F32)) for i in range(2)]
    afh2 = [[es_b.enter_context(nc.sbuf_tensor("afh%d_%d" % (i, j), [64, 512], BF16))
             for i in range(2)] for j in range(2)]
    afl2 = [[es_b.enter_context(nc.sbuf_tensor("afl%d_%d" % (i, j), [64, 512], BF16))
             for i in range(2)] for j in range(2)]
    lb2 = es_b.enter_context(nc.sbuf_tensor("lb2", [128, 512], F32))
    woAh = es_b.enter_context(nc.sbuf_tensor("woAh", [64, H], BF16))
    woAl = es_b.enter_context(nc.sbuf_tensor("woAl", [64, H], BF16))
    woBh = es_b.enter_context(nc.sbuf_tensor("woBh", [64, H], BF16))
    woBl = es_b.enter_context(nc.sbuf_tensor("woBl", [64, H], BF16))

    t_i2 = p.add("gpsimd", lambda g: g.iota(itm2[:], pattern=[[1, 128]], base=127,
                                            channel_multiplier=-1,
                                            allow_small_or_imprecise_dtypes=True),
                 waits=barA, inc=gp)
    t_msk = p.add("vector", lambda v: v.tensor_scalar(trim[:], itm2[:], 127.0, -1e30,
                                                      ALU.is_lt, ALU.mult),
                  waits=[t_i2], inc=ve)

    t_km = p.add("vector", lambda v: (v.memset(Keh[64:66, :], 1.0),
                                      v.memset(Kel[64:66, :], 0.0),
                                      v.memset(qpat[64:66, :], 1.0))[-1], waits=barA, inc=ve)
    t_ki = p.add("gpsimd", lambda g: (
        g.iota(Keh[64:65, :], pattern=[[0, S // 128], [1, 128]], base=0,
               channel_multiplier=0, allow_small_or_imprecise_dtypes=True),
        g.iota(qpat[64:66, 0:512], pattern=[[1, 512]], base=0,
               channel_multiplier=0, allow_small_or_imprecise_dtypes=True),
        g.iota(dcol[:], pattern=[[1, 19]], base=0, channel_multiplier=0,
               allow_small_or_imprecise_dtypes=True))[-1],
                 waits=[t_km], inc=gp)
    t_kmi = p.add("vector", lambda v: v.memset(qpat[64:65, 0:512], 1.0), waits=[t_ki], inc=ve)
    qpat_done = NONE
    t_vones = p.add("vector", lambda v: v.memset(
        Vp.ap().rearrange("p (t h o) -> p t h o", t=NTC, h=2)[:, :, :, 64:65], 1.0), inc=ve)

    stg_free = [NONE, NONE]
    vp_ready = [NONE] * NTC
    attn_rows_free = NONE
    afq_free = [[NONE, NONE], [NONE, NONE]]
    afq_ready_q = [NONE, NONE]
    PT_free = [NONE, NONE]
    psA01_free = [NONE, NONE]
    psA2_free = NONE
    pv_psA2_wait = [NONE]
    psB_free = [t_oaz, t_oaz]
    oa_last = [t_oaz] * NTC
    vgw_free = NONE
    vgs_free = NONE
    wq_free = NONE
    wqs_free = NONE
    wo_free = NONE
    wo_ready = NONE
    slope_rdy = [NONE, NONE]
    bt_ready = [NONE, NONE]
    bt_free = [NONE, NONE]
    lrow_free = NONE
    op_pend = [None]
    op_last = [NONE]
    aftmp_free = NONE

    for P in range(n_pack):
        pg4 = P // 4
        if P % 2 == 0:
            pg2 = P // 2
            vsrc = wqkv_d[:, (pg2 // 2) * 1024 + (pg2 % 2) * 128:(pg2 // 2) * 1024 + (pg2 % 2) * 128 + 128]
            t_vh = p.add("gpsimd", lambda gg, vsrc=vsrc: gg.dma_start(
                out=vgh.ap().rearrange("p (c k) -> p c k", c=NHC),
                in_=vsrc.rearrange("(c p) k -> p c k", p=128)),
                waits=[vgs_free] + barA, inc=s_vh, inck=16)
            t_vsl = NONE
            for vq in range(2):
                sb = vq % 2
                t_vs = p.add("sync", lambda sy, vsrc=vsrc, vq=vq, sb=sb: sy.dma_start(
                    out=stg2[sb].ap().rearrange("p (c k) -> p c k", c=4),
                    in_=vsrc.rearrange("(c p) k -> p c k", p=128)[:, vq * 4:(vq + 1) * 4, :]),
                    waits=[stg_free[sb]] + barA, inc=s_st2[sb], inck=16)
                t_vsl = p.add("gpsimd", lambda gg, vq=vq, sb=sb: gg.tensor_sub(
                    vgl[:, vq * 512:(vq + 1) * 512], stg2[sb][:],
                    vgh[:, vq * 512:(vq + 1) * 512]),
                    waits=[t_vs, t_vh, vgs_free, t_vsl], inc=gp)
                stg_free[sb] = t_vsl
            vgw_free = t_vsl
            last_tv = NONE
            for tc in range(NTC):
                def mmv(t, tc=tc):
                    ins = None
                    for c in range(NHC):
                        ins = mm3(t, psA[2][:, 0:128],
                                  xnh[:, c * S + tc * 128:c * S + tc * 128 + 128],
                                  xnl[:, c * S + tc * 128:c * S + tc * 128 + 128],
                                  vgh[:, c * 128:(c + 1) * 128], vgl[:, c * 128:(c + 1) * 128],
                                  c == 0, c == NHC - 1)
                    return ins
                t_v = p.add("tensor", mmv, waits=[xn_ready, vgw_free, pv_psA2_wait[0]], inc=te)
                last_tv = t_v

                def evv(v, tc=tc):
                    dst = Vp[:, tc * 130:tc * 130 + 130].rearrange("p (h o) -> p h o", h=2)[:, :, 0:64]
                    src = psA[2][:, 0:128].rearrange("p (h o) -> p h o", h=2)
                    return v.tensor_copy(dst, src)
                psA2_free = p.add("vector", evv, waits=[t_v, t_vones], inc=ve)
                pv_psA2_wait[0] = psA2_free
                vp_ready[tc] = psA2_free
            vgs_free = last_tv

        qsrc = wqkv_d[:, pg4 * 1024 + 256 + (P % 4) * 192:pg4 * 1024 + 256 + (P % 4) * 192 + 192]
        t_qh = p.add("gpsimd", lambda gg, qsrc=qsrc: gg.dma_start(
            out=wqh.ap().rearrange("p (c k) -> p c k", c=NHC),
            in_=qsrc.rearrange("(c p) k -> p c k", p=128)),
            waits=[wqs_free] + barA, inc=s_qh, inck=16)
        wq_lo = NONE
        for qh in range(3):
            sb = qh % 2
            t_qs = p.add("sync", lambda sy, qsrc=qsrc, qh=qh, sb=sb: sy.dma_start(
                out=stg2[sb].ap().rearrange("p (c k) -> p c k", c=8),
                in_=qsrc.rearrange("(c p) (q k) -> p c q k", p=128, q=3)[:, :, qh, :]),
                waits=[stg_free[sb]] + barA, inc=s_st2[sb], inck=16)
            wq_lo = p.add("gpsimd", lambda gg, qh=qh, sb=sb: gg.tensor_sub(
                wql.ap().rearrange("p (c k) -> p c k", c=NHC)
                .rearrange("p c (q k) -> p c q k", q=3)[:, :, qh, :],
                stg2[sb][:].rearrange("p (c k) -> p c k", c=8),
                wqh.ap().rearrange("p (c k) -> p c k", c=NHC)
                .rearrange("p c (q k) -> p c q k", q=3)[:, :, qh, :]),
                waits=[t_qs, t_qh, wqs_free, wq_lo], inc=gp)
            stg_free[sb] = wq_lo
        wq_ready = wq_lo
        t_ohA = p.add("gpsimd", lambda gg, P=P: gg.dma_start(
            out=woAh[:], in_=wout_d[P * 128:P * 128 + 64, :]),
            waits=[wo_free] + barA, inc=s_wo[0], inck=16)
        t_ohB = p.add("gpsimd", lambda gg, P=P: gg.dma_start(
            out=woBh[:], in_=wout_d[P * 128 + 64:P * 128 + 128, :]),
            waits=[wo_free] + barA, inc=s_wo[1], inck=16)
        wo_ready = NONE
        for wv in range(4):
            half = wv // 2
            sb = wv % 2
            wsrc = wout_d[P * 128 + half * 64:P * 128 + half * 64 + 64,
                          (wv % 2) * 512:(wv % 2) * 512 + 512]
            t_ws = p.add("sync", lambda sy, wsrc=wsrc, sb=sb: sy.dma_start(
                out=stg2[sb][0:64, 0:512], in_=wsrc),
                waits=[stg_free[sb]] + barA, inc=s_st2[sb], inck=16)
            dst = (woAl if half == 0 else woBl)
            hsrc = (woAh if half == 0 else woBh)
            wo_ready = p.add("gpsimd", lambda gg, dst=dst, hsrc=hsrc, wv=wv, sb=sb: gg.tensor_sub(
                dst[:, (wv % 2) * 512:(wv % 2) * 512 + 512], stg2[sb][0:64, 0:512],
                hsrc[:, (wv % 2) * 512:(wv % 2) * 512 + 512]),
                waits=[t_ws, (t_ohA if half == 0 else t_ohB), wo_free, wo_ready], inc=gp)
            stg_free[sb] = wo_ready

        slA = float(SLOPES[2 * P])
        slB = float(SLOPES[2 * P + 1])
        for hx, (Qxh, Qxl, bt, sl) in enumerate(((QeAh, QeAl, btA, slA),
                                                 (QeBh, QeBl, btB, slB))):
            t_bt = p.add("vector", lambda v, bt=bt, sl=sl: v.tensor_scalar(
                bt[:], dcol[:], 128.0 * sl, -15.0 * 128.0 * sl + EXP_SHIFT, ALU.mult, ALU.add),
                waits=[t_kmi, bt_free[hx]], inc=ve)
            bt_ready[hx] = t_bt
            t_qs1 = p.add("vector", lambda v, hx=hx, sl=sl: v.memset(qsv[hx][64:66, 0:1], -sl),
                          waits=[slope_rdy[hx]], inc=ve)
            t_qs2 = p.add("vector", lambda v, hx=hx, sl=sl: v.memset(qsv[hx][64:65, 0:1], sl),
                          waits=[t_qs1], inc=ve)

            def extq(v, Qxh=Qxh, hx=hx):
                ins = None
                for qb5 in range(S // 512):
                    ins = v.tensor_scalar(Qxh[64:66, qb5 * 512:(qb5 + 1) * 512],
                                          qpat[64:66, 0:512], qsv[hx][64:66, 0:1], None, ALU.mult)
                return ins
            t_eh = p.add("vector", extq, waits=[t_qs2, attn_rows_free, t_kmi], inc=ve)

            def extql(v, Qxh=Qxh, Qxl=Qxl, hx=hx):
                ins = None
                for qb5 in range(S // 512):
                    ins = v.scalar_tensor_tensor(Qxl[64:66, qb5 * 512:(qb5 + 1) * 512],
                                                 qpat[64:66, 0:512], qsv[hx][64:66, 0:1],
                                                 Qxh[64:66, qb5 * 512:(qb5 + 1) * 512],
                                                 ALU.mult, ALU.subtract)
                return ins
            slope_rdy[hx] = p.add("vector", extql, waits=[t_eh], inc=ve)

        qk_ready = NONE
        for tb in range(S // 512):
            def mmq(t, tb=tb):
                ins = None
                for c in range(NHC):
                    ins = mm3(t, psA[0][:, 0:512],
                              wqh[:, c * 192:c * 192 + 128], wql[:, c * 192:c * 192 + 128],
                              xnh[:, c * S + tb * 512:c * S + tb * 512 + 512],
                              xnl[:, c * S + tb * 512:c * S + tb * 512 + 512],
                              c == 0, c == NHC - 1)
                return ins
            t_q = p.add("tensor", mmq, waits=[wq_ready, psA01_free[0], xn_ready], inc=te)

            def mmk(t, tb=tb):
                ins = None
                for c in range(NHC):
                    ins = mm3(t, psA[1][0:64, 0:512],
                              wqh[:, c * 192 + 128:c * 192 + 192], wql[:, c * 192 + 128:c * 192 + 192],
                              xnh[:, c * S + tb * 512:c * S + tb * 512 + 512],
                              xnl[:, c * S + tb * 512:c * S + tb * 512 + 512],
                              c == 0, c == NHC - 1)
                return ins
            t_k = p.add("tensor", mmk, waits=[psA01_free[1]], inc=te)
            wqs_free = t_k

            def evqh(v, tb=tb):
                v.tensor_scalar(QeAh[0:64, tb * 512:(tb + 1) * 512], psA[0][0:64, 0:512], 0.125, None, ALU.mult)
                return v.tensor_scalar(QeBh[0:64, tb * 512:(tb + 1) * 512], psA[0][64:128, 0:512], 0.125, None, ALU.mult)
            t_eqh = p.add("vector", evqh, waits=[t_q, attn_rows_free], inc=ve)

            def evql(v, tb=tb):
                v.scalar_tensor_tensor(QeAl[0:64, tb * 512:(tb + 1) * 512], psA[0][0:64, 0:512],
                                       0.125, QeAh[0:64, tb * 512:(tb + 1) * 512], ALU.mult, ALU.subtract)
                return v.scalar_tensor_tensor(QeBl[0:64, tb * 512:(tb + 1) * 512], psA[0][64:128, 0:512],
                                              0.125, QeBh[0:64, tb * 512:(tb + 1) * 512], ALU.mult, ALU.subtract)
            psA01_free[0] = p.add("vector", evql, waits=[t_eqh], inc=ve)

            t_ekh = p.add("vector", lambda v, tb=tb: v.tensor_copy(
                Keh[0:64, tb * 512:(tb + 1) * 512], psA[1][0:64, 0:512]),
                waits=[t_k, attn_rows_free], inc=ve)
            psA01_free[1] = p.add("vector", lambda v, tb=tb: v.scalar_tensor_tensor(
                Kel[0:64, tb * 512:(tb + 1) * 512], psA[1][0:64, 0:512], 1.0,
                Keh[0:64, tb * 512:(tb + 1) * 512], ALU.mult, ALU.subtract),
                waits=[t_ekh], inc=ve)
            qk_ready = psA01_free[1]
        qk_ready2 = psA01_free[0]

        last_st = NONE
        for qb in range(NQB):
            kcmax = 4 * qb + 4
            for hx in range(2):
                Qeh_, Qel_ = (QeAh, QeAl) if hx == 0 else (QeBh, QeBl)
                btx = btA if hx == 0 else btB
                hh = P % 2
                pv_pend = None
                for kc in range(kcmax):
                    bst = kc % 2

                    def mmst(t, kc=kc, qb=qb, Qeh_=Qeh_, Qel_=Qel_, bst=bst):
                        return mm3(t, psA[bst][:, 0:512],
                                   Keh[0:66, kc * 128:(kc + 1) * 128],
                                   Kel[0:66, kc * 128:(kc + 1) * 128],
                                   Qeh_[0:66, qb * 512:(qb + 1) * 512],
                                   Qel_[0:66, qb * 512:(qb + 1) * 512],
                                   True, True)
                    t_st = p.add("tensor", mmst,
                                 waits=[qk_ready, qk_ready2, psA01_free[bst],
                                        slope_rdy[hx], t_ki],
                                 inc=te)
                    last_st = t_st

                    j = kc - 4 * qb
                    st_rdy = t_st
                    if j >= 0:
                        def premask(v, j=j, bst=bst):
                            if j > 0:
                                v.tensor_scalar(psA[bst][:, 0:128 * j], psA[bst][:, 0:128 * j],
                                                -1e30, None, ALU.add)
                            return v.tensor_tensor(psA[bst][:, 128 * j:128 * j + 128],
                                                   psA[bst][:, 128 * j:128 * j + 128],
                                                   trim[:], ALU.add)
                        st_rdy = p.add("vector", premask, waits=[t_st, t_msk], inc=ve)
                    dd = kc - 4 * qb + 15
                    c_exp = p.add("scalar", lambda s, bst=bst, kc=kc, btx=btx, dd=dd: s.activation(
                        PT[kc % 2][:], psA[bst][:, 0:512], AF.Exp, bias=btx[:, dd:dd + 1], scale=1.0),
                        waits=[st_rdy, PT_free[kc % 2], t_bexp, bt_ready[hx]], inc=sc)
                    bt_free[hx] = c_exp
                    psA01_free[bst] = c_exp

                    def mk_pv(kc, hh, first, last, pt_ready):
                        def mmpv(t, kc=kc, hh=hh, first=first, last=last):
                            return t.matmul(psA[2][0:65, 0:512],
                                            Vp[:, kc * 130 + hh * 65:kc * 130 + hh * 65 + 65],
                                            PT[kc % 2][:],
                                            start=first, stop=last)
                        def emit():
                            t_pv = p.add("tensor", mmpv,
                                         waits=[pt_ready, vp_ready[kc],
                                                (pv_psA2_wait[0] if first else NONE)],
                                         inc=te)
                            PT_free[kc % 2] = t_pv
                            return t_pv
                        return emit
                    if pv_pend is not None:
                        pv_pend()
                    pv_pend = mk_pv(kc, hh, kc == 0, kc == kcmax - 1, c_exp)
                last_pv = pv_pend()
                t_r = p.add("vector", lambda v: v.reciprocal(lb2[64:65, 0:512], psA[2][64:65, 0:512]),
                            waits=[PT_free[(kcmax - 1) % 2], lrow_free], inc=ve)
                t_lb = p.add("tensor", lambda t: t.matmul(psA[3][0:64, 0:512], onesP[64:65, 0:64],
                                                          lb2[64:65, 0:512], start=True, stop=True),
                             waits=[t_r, psA3_free, t_ones], inc=te)
                lrow_free = t_lb
                t_lbe = p.add("vector", lambda v: v.tensor_copy(lb2[0:64, 0:512], psA[3][0:64, 0:512]),
                              waits=[t_lb, qpat_done], inc=ve)
                psA3_free = t_lbe

                def nrm(v, hx=hx, qb=qb):
                    return v.scalar_tensor_tensor(aftmp[:], psA[2][0:64, 0:512], 1.0,
                                                  lb2[0:64, 0:512], ALU.mult, ALU.mult)
                t_nrm = p.add("vector", nrm, waits=[t_lbe, aftmp_free], inc=ve)
                psA2_free = t_nrm
                pv_psA2_wait[0] = t_nrm
                t_afh = p.add("vector", lambda v, hx=hx, qb=qb: v.tensor_copy(
                    afh2[qb % 2][hx][:], aftmp[:]), waits=[t_nrm, afq_free[qb % 2][hx]], inc=ve)
                aftmp_free = p.add("vector", lambda v, hx=hx, qb=qb: v.scalar_tensor_tensor(
                    afl2[qb % 2][hx][:], aftmp[:], 1.0, afh2[qb % 2][hx][:], ALU.mult, ALU.subtract),
                    waits=[t_afh], inc=ve)
                afq_ready_q[qb % 2] = aftmp_free

            def mk_outproj(qb, P=P):
                def emit():
                    for tcl in range(4):
                        tc = qb * 4 + tcl
                        bb = tcl % 2

                        def mmo(t, tcl=tcl, bb=bb, qb=qb):
                            ins = None
                            for nh in range(2):
                                mm3(t, psB[bb][:, nh * 512:(nh + 1) * 512],
                                    afh2[qb % 2][0][:, tcl * 128:(tcl + 1) * 128],
                                    afl2[qb % 2][0][:, tcl * 128:(tcl + 1) * 128],
                                    woAh[:, nh * 512:(nh + 1) * 512],
                                    woAl[:, nh * 512:(nh + 1) * 512], True, False)
                                ins = mm3(t, psB[bb][:, nh * 512:(nh + 1) * 512],
                                          afh2[qb % 2][1][:, tcl * 128:(tcl + 1) * 128],
                                          afl2[qb % 2][1][:, tcl * 128:(tcl + 1) * 128],
                                          woBh[:, nh * 512:(nh + 1) * 512],
                                          woBl[:, nh * 512:(nh + 1) * 512], False, True)
                            return ins
                        t_o = p.add("tensor", mmo, waits=[afq_ready_q[qb % 2], wo_ready, psB_free[bb]], inc=te)
                        if tcl == 3:
                            afq_free[qb % 2] = [t_o, t_o]
                            op_last[0] = t_o

                        def evo(v, tc=tc, bb=bb):
                            return v.tensor_tensor(oa[:, tc * H:(tc + 1) * H], psB[bb][:],
                                                   oa[:, tc * H:(tc + 1) * H], ALU.add)
                        psB_free[bb] = p.add("vector", evo, waits=[t_o, oa_last[tc]], inc=ve)
                        oa_last[tc] = psB_free[bb]
                return emit
            if op_pend[0] is not None:
                op_pend[0]()
            op_pend[0] = mk_outproj(qb)
            if qb == NQB - 1:
                op_pend[0]()
                op_pend[0] = None
                wo_free = op_last[0]
        attn_rows_free = last_st
    es_b.close()

    barB = barrier()
    es_c = ExitStack()
    wgh = es_c.enter_context(nc.sbuf_tensor("wgh", [128, NHC * IEXP], BF16))
    wgl = es_c.enter_context(nc.sbuf_tensor("wgl", [128, NHC * IEXP], BF16))
    wuh = es_c.enter_context(nc.sbuf_tensor("wuh", [128, NHC * IEXP], BF16))
    wul = es_c.enter_context(nc.sbuf_tensor("wul", [128, NHC * IEXP], BF16))
    wdh = es_c.enter_context(nc.sbuf_tensor("wdh", [128, 4 * H], BF16))
    wdl = es_c.enter_context(nc.sbuf_tensor("wdl", [128, 4 * H], BF16))
    acth = [es_c.enter_context(nc.sbuf_tensor("acth%d" % i, [128, 4 * 512], BF16)) for i in range(2)]
    actl = [es_c.enter_context(nc.sbuf_tensor("actl%d" % i, [128, 4 * 512], BF16)) for i in range(2)]
    tact = [es_c.enter_context(nc.sbuf_tensor("tact%d" % i, [128, 512], F32)) for i in range(2)]
    sg_t = [es_c.enter_context(nc.sbuf_tensor("sg%d" % i, [128, 512], F32)) for i in range(2)]

    wg_free = NONE
    wu_free = NONE
    wd_free = NONE
    g_ps_free = [psA01_free[0], psA01_free[1]]
    u_ps_free = [psA2_free, psA3_free]
    d_ps_free = [psB_free[0], psB_free[1]]
    act_free = [NONE, NONE]
    sg_free = [NONE, NONE]
    tact_free = [NONE, NONE]
    pend_wd = [None]

    elist = list(exp_ids) if exp_ids is not None else list(range(n_exp))

    wready = {}

    def emit_w(idx):
        def one(src_d, htile, ltile, s_hi, s_st_unused, mfree, C, W, e):
            nonlocal_stg = None
            t_hi = p.add("gpsimd", lambda gg, e=e, src_d=src_d, htile=htile, C=C: gg.dma_start(
                out=htile.ap().rearrange("p (c i) -> p c i", c=C),
                in_=src_d[e].rearrange("(c p) i -> p c i", p=128)),
                waits=[mfree] + barB, inc=s_hi, inck=16)
            lo_done = NONE
            cpq = 512 // W if W <= 512 else 0
            for hf in range(8):
                sb = hf % 2
                if W == 512:
                    src_ap = src_d[e].rearrange("(c p) i -> p c i", p=128)[:, hf:hf + 1, :]
                    dst_ap = stg2[sb].ap().rearrange("p (c i) -> p c i", c=1)
                else:
                    src_ap = src_d[e].rearrange("(c p) (q i) -> p c q i", p=128, q=2)[:, hf // 2, hf % 2, :]
                    dst_ap = stg2[sb][:]
                t_st = p.add("sync", lambda sy, src_ap=src_ap, dst_ap=dst_ap: sy.dma_start(
                    out=dst_ap, in_=src_ap),
                    waits=[stg_free[sb]] + barB, inc=s_st2[sb], inck=16)
                lo_done = p.add("gpsimd", lambda gg, htile=htile, ltile=ltile, hf=hf, sb=sb: gg.tensor_sub(
                    ltile[:, hf * 512:(hf + 1) * 512], stg2[sb][:],
                    htile[:, hf * 512:(hf + 1) * 512]),
                    waits=[t_st, t_hi, mfree, lo_done], inc=gp)
                stg_free[sb] = lo_done
            return lo_done
        e = elist[idx]
        return {
            "g": lambda mfree: one(eg_d, wgh, wgl, s_gh, s_wg, mfree, 8, 512, e),
            "u": lambda mfree: one(eu_d, wuh, wul, s_uh, s_wu, mfree, 8, 512, e),
            "d": lambda mfree: one(ed_d, wdh, wdl, s_dh, s_wd, mfree, 4, 1024, e),
        }

    w0 = emit_w(0)
    wready["g"] = w0["g"](NONE)
    wready["u"] = w0["u"](NONE)
    wready["d"] = w0["d"](NONE)

    down_pend = [None]

    def mk_down(e, tbp, ab, act_tok, wd_tok):
        def emit():
            for tcl in range(4):
                tc = tbp * 4 + tcl
                db = tc % 2

                def mmd(t, tcl=tcl, ab=ab, db=db):
                    ins = None
                    for nh in range(2):
                        for ic in range(4):
                            ins = mm3(t, psB[db][:, nh * 512:(nh + 1) * 512],
                                      acth[ab][:, ic * 512 + tcl * 128:ic * 512 + tcl * 128 + 128],
                                      actl[ab][:, ic * 512 + tcl * 128:ic * 512 + tcl * 128 + 128],
                                      wdh[:, ic * H + nh * 512:ic * H + nh * 512 + 512],
                                      wdl[:, ic * H + nh * 512:ic * H + nh * 512 + 512],
                                      ic == 0, ic == 3)
                    return ins
                t_d = p.add("tensor", mmd, waits=[act_tok, wd_tok, d_ps_free[db]], inc=te)
                if tcl == 3:
                    act_free[ab] = t_d
                    down_last[0] = t_d

                if e < NE:
                    def evd(v, tc=tc, db=db, e=e):
                        return v.scalar_tensor_tensor(oa[:, tc * H:(tc + 1) * H], psB[db][:],
                                                      wde[:, tc * 32 + e:tc * 32 + e + 1],
                                                      oa[:, tc * H:(tc + 1) * H], ALU.mult, ALU.add)
                    d_ps_free[db] = p.add("vector", evd, waits=[t_d, oa_last[tc], wde_ready], inc=ve)
                else:
                    def evd(v, tc=tc, db=db):
                        return v.tensor_tensor(oa[:, tc * H:(tc + 1) * H], psB[db][:],
                                               oa[:, tc * H:(tc + 1) * H], ALU.add)
                    d_ps_free[db] = p.add("vector", evd, waits=[t_d, oa_last[tc]], inc=ve)
                oa_last[tc] = d_ps_free[db]
        return emit

    down_last = [NONE]
    for idx, e in enumerate(elist):
        wg_now, wu_now, wd_now = wready["g"], wready["u"], wready["d"]
        for tbp in range(NTBP):
            ab = tbp % 2
            act_ready = NONE
            for ic in range(4):
                ib = ic % 2

                def mmgp(t, tbp=tbp, ic=ic, ib=ib):
                    ins = None
                    for c in range(NHC):
                        ins = mm3(t, psA[ib][:, 0:512],
                                  wgh[:, c * IEXP + ic * 128:c * IEXP + ic * 128 + 128],
                                  wgl[:, c * IEXP + ic * 128:c * IEXP + ic * 128 + 128],
                                  xnh[:, c * S + tbp * 512:c * S + tbp * 512 + 512],
                                  xnl[:, c * S + tbp * 512:c * S + tbp * 512 + 512],
                                  c == 0, c == NHC - 1)
                    return ins
                t_g = p.add("tensor", mmgp, waits=[wg_now, g_ps_free[ib], xn_ready], inc=te)

                def mmup(t, tbp=tbp, ic=ic, ib=ib):
                    ins = None
                    for c in range(NHC):
                        ins = mm3(t, psA[2 + ib][:, 0:512],
                                  wuh[:, c * IEXP + ic * 128:c * IEXP + ic * 128 + 128],
                                  wul[:, c * IEXP + ic * 128:c * IEXP + ic * 128 + 128],
                                  xnh[:, c * S + tbp * 512:c * S + tbp * 512 + 512],
                                  xnl[:, c * S + tbp * 512:c * S + tbp * 512 + 512],
                                  c == 0, c == NHC - 1)
                    return ins
                t_u = p.add("tensor", mmup, waits=[wu_now, u_ps_free[ib]], inc=te)

                if ic == 0 and down_pend[0] is not None:
                    down_pend[0]()
                    down_pend[0] = None
                    if wready.get("d") is None and pend_wd[0] is not None:
                        wready["d"] = pend_wd[0][0]["d"](down_last[0])
                        wd_now = wready["d"]
                        pend_wd[0] = None

                c_si = p.add("scalar", lambda s, ib=ib: s.activation(sg_t[ib][:], psA[ib][:, 0:512], AF.Silu),
                             waits=[t_g, sg_free[ib]], inc=sc)
                g_ps_free[ib] = c_si

                tb_ = ic % 2
                v_t1 = p.add("vector", lambda v, ib=ib, tb_=tb_: v.tensor_mul(
                    tact[tb_][:], sg_t[ib][:], psA[2 + ib][:, 0:512]),
                    waits=[c_si, t_u, tact_free[tb_]], inc=ve)
                u_ps_free[ib] = v_t1
                sg_free[ib] = v_t1
                v_ah = p.add("scalar", lambda s, ab=ab, ic=ic, tb_=tb_: s.copy(
                    acth[ab][:, ic * 512:(ic + 1) * 512], tact[tb_][:]),
                    waits=[v_t1, (act_free[ab] if ic == 0 else NONE)], inc=sc)
                v_al = p.add("vector", lambda v, ab=ab, ic=ic, tb_=tb_: v.scalar_tensor_tensor(
                    actl[ab][:, ic * 512:(ic + 1) * 512], tact[tb_][:], 1.0,
                    acth[ab][:, ic * 512:(ic + 1) * 512], ALU.mult, ALU.subtract),
                    waits=[v_ah], inc=ve)
                tact_free[tb_] = v_al
                act_ready = v_al

                if idx + 1 < len(elist) and tbp == NTBP - 1 and ic == 3:
                    wn = emit_w(idx + 1)
                    next_g = wn["g"](t_g)
                    next_u = wn["u"](t_u)
                    pend_wd[0] = (wn, None)

            down_pend[0] = mk_down(e, tbp, ab, act_ready, wd_now)
        if idx + 1 < len(elist):
            wready = {"g": next_g, "u": next_u, "d": None}
    if down_pend[0] is not None:
        down_pend[0]()
        down_pend[0] = None

    es_c.close()
    barC = barrier()
    xt4 = [es.enter_context(nc.sbuf_tensor("xt4%d" % i, [128, H], F32)) for i in range(2)]
    res_t = [es.enter_context(nc.sbuf_tensor("res%d" % i, [128, H], F32)) for i in range(2)]
    res_free = [NONE, NONE]
    x2_free = [NONE, NONE]
    for tc in range(NTC):
        b = tc % 2
        t_x = p.add("sync", lambda sy, tc=tc, b=b: sy.dma_start(
            out=xt4[b][:], in_=x_d[tc * 128:(tc + 1) * 128, :]),
            waits=[x2_free[b]] + barC, inc=s_x[b], inck=16)

        def radd(v, tc=tc, b=b):
            return v.tensor_tensor(res_t[b][:], oa[:, tc * H:(tc + 1) * H], xt4[b][:], ALU.add)
        t_r = p.add("vector", radd, waits=[t_x, oa_last[tc], res_free[b]], inc=ve)
        x2_free[b] = t_r

        res_free[b] = p.add("sync", lambda sy, tc=tc, b=b: sy.dma_start(
            out=out_d[tc * 128:(tc + 1) * 128, :], in_=res_t[b][:]),
            waits=[t_r], inc=s_o[b], inck=16)
    p.add("sync", lambda sy: sy.nop(), waits=[(s_o[0], s_o[0].n), (s_o[1], s_o[1].n)])

    blk = es.enter_context(nc.Block())
    p.emit(blk)
    es.close()
    return nc


def _prep_weights(W_qkv, W_out, gate_w, gate_bias, eg, eu, edn, sg, su, sd):
    cols = []
    for g in range(4):
        for k in range(4 * g, 4 * g + 4):
            cols.extend(range(3072 + 64 * k, 3072 + 64 * k + 64))
        for Pp in range(4 * g, 4 * g + 4):
            cols.extend(range(64 * (2 * Pp), 64 * (2 * Pp) + 64))
            cols.extend(range(64 * (2 * Pp + 1), 64 * (2 * Pp + 1) + 64))
            cols.extend(range(2048 + 64 * Pp, 2048 + 64 * Pp + 64))
    wqkv_re = np.ascontiguousarray(W_qkv[:, cols])
    wout = np.ascontiguousarray(W_out)
    gwT = np.ascontiguousarray(gate_w.T)
    gb = np.ascontiguousarray(gate_bias.reshape(1, 32))
    sgT = sg.T
    suT = su.T
    sdT = sd.T
    eg_all = np.ascontiguousarray(np.concatenate(
        [eg.transpose(0, 2, 1), sgT[None, :, 0:512], sgT[None, :, 512:1024]], axis=0))
    eu_all = np.ascontiguousarray(np.concatenate(
        [eu.transpose(0, 2, 1), suT[None, :, 0:512], suT[None, :, 512:1024]], axis=0))
    ed_all = np.ascontiguousarray(np.concatenate(
        [edn.transpose(0, 2, 1), sdT[None, 0:512, :], sdT[None, 512:1024, :]], axis=0))
    return wqkv_re, wout, gwT, gb, eg_all, eu_all, ed_all


def kernel(L0_attn_inputs, L0_attn_W_qkv, L0_attn_W_out, L0_ffn_gate_w,
           L0_ffn_gate_bias, L0_ffn_expert_gate, L0_ffn_expert_up,
           L0_ffn_expert_down, L0_ffn_shared_gate, L0_ffn_shared_up,
           L0_ffn_shared_down):
    x = np.ascontiguousarray(np.asarray(L0_attn_inputs, dtype=np.float32))
    wqkv_re, wout, gwT, gb, eg_all, eu_all, ed_all = _prep_weights(
        np.asarray(L0_attn_W_qkv, np.float32), np.asarray(L0_attn_W_out, np.float32),
        np.asarray(L0_ffn_gate_w, np.float32), np.asarray(L0_ffn_gate_bias, np.float32),
        np.asarray(L0_ffn_expert_gate, np.float32), np.asarray(L0_ffn_expert_up, np.float32),
        np.asarray(L0_ffn_expert_down, np.float32), np.asarray(L0_ffn_shared_gate, np.float32),
        np.asarray(L0_ffn_shared_up, np.float32), np.asarray(L0_ffn_shared_down, np.float32))

    nc = build_program()
    in_maps = []
    for b in range(8):
        in_maps.append({
            "x": x[b], "wqkv": wqkv_re, "wout": wout, "gwT": gwT, "gb": gb,
            "eg": eg_all, "eu": eu_all, "ed": ed_all,
        })
    res = run_bass_kernel_spmd(nc, in_maps, core_ids=list(range(8)))
    return np.stack([r["out"] for r in res.results], axis=0)
